# version 21 (final)
# baseline (speedup 1.0000x reference)
import numpy as np
import ml_dtypes
import contextlib
import concourse.bass as bass
import concourse.mybir as mybir
from concourse.bass_utils import run_bass_kernel_spmd

F32 = mybir.dt.float32
BF16 = mybir.dt.bfloat16
ALU = mybir.AluOpType
AF = mybir.ActivationFunctionType
AXX = mybir.AxisListType.X

NCORES = 8
TCORE = 2048
D = 4096
DKB = 32
DI = 512
DB = 4
FF = 2048
FB = 16
NL = 4
TC0, NCH0 = 256, 8
TC1, NCH1 = 512, 4
EPS = 1e-5
NJ = TC0 // 128

P_LN1W, P_LN1B, P_PRE, P_PIM, P_LCR, P_LCI, P_SCL, P_LN2W, P_LN2B, P_B2 = range(10)
NPAR = 10

DEBUG_TAPS = False
SCHED = 1
STAGE = 3
_LAST_RES = None


def build_nc():
    nc = bass.Bass("TRN2", target_bir_lowering=False, debug=False,
                   detect_race_conditions=False)
    Ti = lambda n, s: nc.dram_tensor(n, s, F32, kind="ExternalInput").ap()

    xc = Ti("xc", [NCH0, 128, DKB, TC0])
    win_t = Ti("win_t", [128, 16384])
    w1_t = Ti("w1_t", [NL, 128, 8192])
    w2_t = Ti("w2_t", [NL, 128, 8192])
    wout_t = Ti("wout_t", [128, DKB, DB * 128])
    wbase_t = Ti("wbase_t", [DKB, 128, DKB * 128])
    wg_t = Ti("wg_t", [128, DKB, 8])
    a_t = Ti("a_t", [128, DKB, 128])
    bp_t = Ti("bp_t", [128, DKB, 128])
    pch = Ti("pch", [128, NL * NPAR * DB])
    b1p = Ti("b1p", [128, NL * FB])
    binp = Ti("binp", [128, DB])
    lnlwi = Ti("lnlwi", [128, DB])
    lnlbi = Ti("lnlbi", [128, DB])
    bbp = Ti("bbp", [128, DKB])
    bop = Ti("bop", [128, DKB])
    identi = Ti("identi", [128, 128])
    sel8i = Ti("sel8i", [8, 128])
    mki = Ti("mki", [128, 29])

    out_t = nc.dram_tensor("out", [DKB, 128, TCORE], F32, kind="ExternalOutput").ap()

    xn_scr = nc.dram_tensor("xn_scr", [NCH0, 128, DKB, TC0], BF16, kind="Internal").ap()
    w_scr = nc.dram_tensor("w_scr", [NL, 2, 128, 8192], BF16, kind="Internal").ap()
    cc_in = nc.dram_tensor("cc_in", [128, 8], F32, kind="Internal").ap()
    cc_out = nc.dram_tensor("cc_out", [8, 128, 8], F32, kind="Internal").ap()

    dbg = {}
    if DEBUG_TAPS:
        dbg["xn0"] = nc.dram_tensor("dbg_xn0", [128, DKB, TC0], BF16, kind="ExternalOutput").ap()
        dbg["h0"] = nc.dram_tensor("dbg_h0", [128, DB, TCORE], F32, kind="ExternalOutput").ap()
        dbg["hwt"] = nc.dram_tensor("dbg_hwt", [128, TCORE], BF16, kind="ExternalOutput").ap()
        dbg["hl"] = nc.dram_tensor("dbg_hl", [NL, 128, DB, TCORE], F32, kind="ExternalOutput").ap()
        dbg["y00"] = nc.dram_tensor("dbg_y00", [128, DB, TC1], BF16, kind="ExternalOutput").ap()
        dbg["hA"] = nc.dram_tensor("dbg_hA", [128, TC0], F32, kind="ExternalOutput").ap()
        dbg["lgs"] = nc.dram_tensor("dbg_lgs", [8, TC0], F32, kind="ExternalOutput").ap()
        dbg["rts"] = nc.dram_tensor("dbg_rts", [128, NJ, 8], F32, kind="ExternalOutput").ap()
        dbg["rt2"] = nc.dram_tensor("dbg_rt2", [128, NJ, 8], F32, kind="ExternalOutput").ap()
        dbg["rwxd"] = nc.dram_tensor("dbg_rwxd", [128, TC0], F32, kind="ExternalOutput").ap()
        dbg["rex"] = nc.dram_tensor("dbg_rex", [128, NJ, 8], F32, kind="ExternalOutput").ap()
        dbg["rcl"] = nc.dram_tensor("dbg_rcl", [128, NJ, 4], F32, kind="ExternalOutput").ap()
        dbg["u0"] = nc.dram_tensor("dbg_u0", [128, DB, TC1], BF16, kind="ExternalOutput").ap()
        dbg["U0"] = nc.dram_tensor("dbg_U0", [128, DB, TC1], BF16, kind="ExternalOutput").ap()
        dbg["V0"] = nc.dram_tensor("dbg_V0", [128, DB, TC1], BF16, kind="ExternalOutput").ap()
        dbg["C0d"] = nc.dram_tensor("dbg_C0d", [128, DB, TC1], BF16, kind="ExternalOutput").ap()
        dbg["S0d"] = nc.dram_tensor("dbg_S0d", [128, DB, TC1], BF16, kind="ExternalOutput").ap()
        dbg["RC0d"] = nc.dram_tensor("dbg_RC0d", [128, DB, TC1], BF16, kind="ExternalOutput").ap()
        dbg["hc"] = nc.dram_tensor("dbg_hc", [128, DB, TCORE], F32, kind="ExternalOutput").ap()

    es = contextlib.ExitStack()
    with es:
        E = es.enter_context
        h = E(nc.sbuf_tensor("h", [128, DB, TCORE], F32))
        hw = E(nc.sbuf_tensor("hw", [128, TCORE], BF16))
        pchs = E(nc.sbuf_tensor("pchs", [128, NL * NPAR * DB], F32))
        b1s = E(nc.sbuf_tensor("b1s", [128, NL * FB], F32))
        bins = E(nc.sbuf_tensor("bins", [128, DB], F32))
        lnlws = E(nc.sbuf_tensor("lnlws", [128, DB], F32))
        lnlbs = E(nc.sbuf_tensor("lnlbs", [128, DB], F32))
        bbs = E(nc.sbuf_tensor("bbs", [128, DKB], F32))
        bos = E(nc.sbuf_tensor("bos", [128, DKB], F32))
        ident = E(nc.sbuf_tensor("ident", [128, 128], F32))
        sel8s = E(nc.sbuf_tensor("sel8s", [8, 128], F32))
        mks = E(nc.sbuf_tensor("mks", [128, 29], F32))
        onesb = E(nc.sbuf_tensor("onesb", [128, 1], BF16))
        ones1 = E(nc.sbuf_tensor("ones1", [1, 128], F32))
        rows = E(nc.sbuf_tensor("rows", [1, 2048], F32))
        tny = E(nc.sbuf_tensor("tny", [128, 256], F32))
        tab = E(nc.sbuf_tensor("tab", [128, 13, DB, 2], F32))
        rtab = E(nc.sbuf_tensor("rtab", [128, 13, DB], F32))
        ccs = E(nc.sbuf_tensor("ccs", [128, 8], F32))
        gth = E(nc.sbuf_tensor("gth", [128, 8, 8], F32))
        bpb = E(nc.sbuf_tensor("bpb", [128, DKB, 128], BF16))

        ps_rows = E(nc.psum_tensor("ps_rows", [1, 1024], F32))
        ps_bc = E(nc.psum_tensor("ps_bc", [128, 1024], F32))
        ps_mm = E(nc.psum_tensor("ps_mm", [128, 2048], F32))

        sem_names = [
            "qCON", "qXC", "qSG0", "qSG1", "qWL", "qP2A", "qP2B", "qP2O", "qPX0", "qPX1", "qSPILL", "qRWX", "qGTH", "qCCIN", "qXNL", "qOUT", "qWSP",
            "cCC", "sCW",
            "vCA", "sSQ", "tSQG", "tST0", "vRW0", "sIV0", "vMI0", "tBC0", "vXN", "tPJ",
            "vEV0", "tTR1", "vRTa", "sRTb", "vRTc", "tTR2", "tRWX", "vRWF", "vHW", "vTIN",
            "vLNC", "sLNQ", "tLNS", "vROW", "sINV", "vMIV", "tLNB", "vNRM", "vSCU",
            "sY1", "vY2", "vHLs",
            "gBAS", "vPREP", "vSCAN", "gSCN", "gCMB", "vH1", "vE", "vZ",
            "vPH", "sPH", "tM1T", "sZEV", "tM2T", "vHF",
            "tP2", "vEV2", "vWC2", "qDBG",
            "pB",
        ]
        S = {n: E(nc.semaphore(n)) for n in sem_names}

        loadlist = [("consts", 10)]
        for j_ in range(5):
            loadlist.append((f"sc{j_}", 1))
        for pc in range(8):
            loadlist.append((f"win{pc}", 1))
        wpieces = [(i, m, pc) for i in range(NL) for m in range(2) for pc in range(4)]
        pi = 0
        for c in range(NCH0):
            loadlist.append((f"x{c}", 1))
            for _ in range(4):
                if pi < len(wpieces):
                    i_, m_, p_ = wpieces[pi]
                    loadlist.append((f"wp{i_}_{m_}_{p_}", 1))
                    pi += 1
        for i in range(NL):
            loadlist.append((f"wl{i}_1", 1))
            loadlist.append((f"wl{i}_2", 1))
        for sl in range(2):
            for m in range(DKB):
                loadlist.append((f"p2w_{sl}_{m}", 2))
                loadlist.append((f"p2o_{sl}_{m}", 1))
                loadlist.append((f"p2x_{sl}_{m}_0", 1))
                loadlist.append((f"p2x_{sl}_{m}_1", 1))
        QX = {}
        _acc = 0
        for k_, n_ in loadlist:
            _acc += n_
            QX[k_] = 16 * _acc

        def pcol(i, j, b):
            o = ((i * NPAR) + j) * DB + b
            return pchs[:, o:o + 1]

        def b1col(i, mi):
            return b1s[:, i * FB + mi:i * FB + mi + 1]

        def w1slot(i):
            return [0, 2][i % 2]

        TNY = {}
        _tn = [0]

        def tcol(name, n=1):
            if name not in TNY:
                TNY[name] = _tn[0]
                _tn[0] += n
                assert _tn[0] <= 256
            return tny[:, TNY[name]:TNY[name] + n]

        tcol("Ea", 8)
        TNY["zz"] = TNY["Ea"]
        tcol("Z", 8)
        TNY["rr"] = TNY["Z"]

        def tabv(j, comp):
            return tab[:, j, :, comp]

        def tabc(j, b, comp):
            return tab[:, j, b, comp:comp + 1]

        def rtabv(j):
            return rtab[:, j, :]

        def rtabc(j, b):
            return rtab[:, j, b:b + 1]

        def emit_ln(role, eng, G, toff, h_slice, epilogue, part="full"):
            nt = TC1
            r_mu = rows[0:1, 0:nt]
            r_t = rows[0:1, 512:512 + nt]
            r_var = rows[0:1, 1024:1024 + nt]
            r_inv = rows[0:1, 1536:1536 + nt]
            if role == "vector":
                if part in ("full", "copy"):
                    eng.tensor_copy(hb[:], h_slice).then_inc(S["vLNC"], 1)
                if part in ("full", "rows", "rows1"):
                    eng.wait_ge(S["tLNS"], G + 1)
                    eng.tensor_scalar_mul(r_mu, ps_rows[0:1, 0:nt], -1.0 / DI)
                    eng.tensor_mul(r_t, r_mu, r_mu)
                    eng.scalar_tensor_tensor(r_var, ps_rows[0:1, 512:512 + nt],
                                             1.0 / DI, r_t, ALU.mult, ALU.subtract)
                    eng.tensor_scalar_add(r_t, r_var, EPS)
                    eng.reciprocal(r_var, r_t).then_inc(S["vROW"], 1)
                if part in ("full", "rows", "rows2"):
                    eng.wait_ge(S["sINV"], G + 1)
                    eng.tensor_mul(r_t, r_mu, r_inv).then_inc(S["vMIV"], 1)
                if part in ("full", "norm"):
                    eng.wait_ge(S["tLNB"], G + 1)
                    bc_i = ps_bc[:, 0:nt].unsqueeze(1).broadcast_to([128, DB, nt])
                    bc_m = ps_bc[:, 512:512 + nt].unsqueeze(1).broadcast_to([128, DB, nt])
                    eng.tensor_mul(xh[:], h_slice, bc_i)
                    eng.tensor_tensor(xh[:], bc_m, xh[:], ALU.add).then_inc(S["vNRM"], 1)
            elif role == "scalar":
                if part in ("full", "stats"):
                    eng.wait_ge(S["vLNC"], G + 1)
                    eng.activation(out=hqb[:], in_=hb[:],
                                   func=AF.Square).then_inc(S["sLNQ"], 1)
                if part in ("full", "bc"):
                    eng.wait_ge(S["vROW"], G + 1)
                    eng.activation(out=r_inv, in_=r_var,
                                   func=AF.Sqrt).then_inc(S["sINV"], 1)
            elif role == "tensor":
                if part in ("full", "stats"):
                    eng.wait_ge(S["vLNC"], G + 1)
                    eng.wait_ge(S["sLNQ"], G + 1)
                    if G > 0:
                        eng.wait_ge(S["vROW"], G)
                    for b in range(DB):
                        eng.matmul(ps_rows[0:1, 0:nt], onesb[:], hb[:, b, :],
                                   start=(b == 0), stop=(b == DB - 1))
                    for b in range(DB):
                        mm = eng.matmul(ps_rows[0:1, 512:512 + nt], onesb[:],
                                        hqb[:, b, :],
                                        start=(b == 0), stop=(b == DB - 1))
                    mm.then_inc(S["tLNS"], 1)
                if part in ("full", "bc"):
                    eng.wait_ge(S["vMIV"], G + 1)
                    if G > 0:
                        eng.wait_ge(S["vNRM"], G)
                    eng.matmul(ps_bc[:, 0:nt], ones1[:], r_inv, start=True, stop=True)
                    eng.matmul(ps_bc[:, 512:512 + nt], ones1[:], r_t,
                               start=True, stop=True).then_inc(S["tLNB"], 1)
            if part in ("full", "bc", "norm"):
                epilogue(role, eng)

        def emit_cplx_mul(eng, dst, xa, xb_, t0, t1, t2, t3):
            eng.tensor_mul(t0, xa[:, 0:4], xb_[:, 0:4])
            eng.tensor_mul(t1, xa[:, 4:8], xb_[:, 4:8])
            eng.tensor_mul(t2, xa[:, 0:4], xb_[:, 4:8])
            eng.tensor_mul(t3, xa[:, 4:8], xb_[:, 0:4])
            eng.tensor_tensor(dst[:, 0:4], t0, t1, ALU.subtract)
            eng.tensor_add(dst[:, 4:8], t2, t3)

        es0 = contextlib.ExitStack()
        with es0:
            E0 = es0.enter_context
            stage = [E0(nc.sbuf_tensor(f"stage{j}", [128, 2048], F32)) for j in range(2)]
            castb = [E0(nc.sbuf_tensor(f"castb{j}", [128, 2048], BF16)) for j in range(4)]
            slot0 = [E0(nc.sbuf_tensor(f"wslot{j}", [128, 8192], BF16)) for j in range(2)]
            xch = E0(nc.sbuf_tensor("xch", [128, DKB, TC0], F32))
            xb = E0(nc.sbuf_tensor("xb", [128, DKB, TC0], BF16))
            xnb = E0(nc.sbuf_tensor("xnb", [128, DKB, TC0], BF16))
            xsqp = [E0(nc.sbuf_tensor(f"xsqp{j}", [128, 8, TC0], BF16)) for j in range(2)]
            ablk = E0(nc.sbuf_tensor("ablk", [128, DKB, 128], BF16))
            gblk = E0(nc.sbuf_tensor("gblk", [128, DKB, 8], BF16))
            hAc = E0(nc.sbuf_tensor("hAc", [128, TC0], F32))
            lgs = E0(nc.sbuf_tensor("lgs", [8, TC0], F32))
            rwf = E0(nc.sbuf_tensor("rwf", [8, TC0], F32))
            rts = E0(nc.sbuf_tensor("rts", [128, NJ, 8], F32))
            eqc = E0(nc.sbuf_tensor("eqc", [128, NJ, 8], F32))
            t8 = E0(nc.sbuf_tensor("t8", [128, NJ, 8], F32))
            t9 = E0(nc.sbuf_tensor("t9", [128, NJ, 8], F32))
            expc = E0(nc.sbuf_tensor("expc", [128, NJ, 8], F32))
            on8 = E0(nc.sbuf_tensor("on8", [128, 8], F32))
            zr8 = E0(nc.sbuf_tensor("zr8", [128, 8], F32))
            rscr = E0(nc.sbuf_tensor("rscr", [128, 8], F32))
            mdc = E0(nc.sbuf_tensor("mdc", [128, NJ], F32))
            m1c = E0(nc.sbuf_tensor("m1c", [128, NJ], F32))
            nm1c = E0(nc.sbuf_tensor("nm1c", [128, NJ], F32))
            m2c = E0(nc.sbuf_tensor("m2c", [128, NJ], F32))
            rcpc = E0(nc.sbuf_tensor("rcpc", [128, NJ], F32))

            def win_ap(mb, kb):
                o = (kb * DB + mb) * 128
                return slot0[o // 8192][:, o % 8192:o % 8192 + 128]

            blk0 = E0(nc.Block("p0"))

            @blk0.sync
            def _(eng):
                for key, _n in loadlist:
                    if key.startswith(("wl", "p2")):
                        continue
                    if key == "consts":
                        for dst, src in [(pchs, pch), (b1s, b1p), (bins, binp),
                                         (lnlws, lnlwi), (lnlbs, lnlbi), (bbs, bbp),
                                         (bos, bop), (ident, identi), (mks, mki),
                                         (sel8s, sel8i)]:
                            eng.dma_start(out=dst[:], in_=src).then_inc(S["qCON"], 16)
                    elif key.startswith("sc"):
                        gp = int(key[2:])
                        if gp >= 2:
                            eng.wait_ge(S["sCW"], gp - 1)
                        srcs = [a_t.rearrange("p a b -> p (a b)")[:, 0:2048],
                                a_t.rearrange("p a b -> p (a b)")[:, 2048:4096],
                                bp_t.rearrange("p a b -> p (a b)")[:, 0:2048],
                                bp_t.rearrange("p a b -> p (a b)")[:, 2048:4096],
                                wg_t.rearrange("p a b -> p (a b)")]
                        dstsz = 2048 if gp < 4 else 256
                        eng.dma_start(out=stage[gp % 2][:, 0:dstsz],
                                      in_=srcs[gp]).then_inc(S[f"qSG{gp % 2}"], 16)
                    elif key.startswith("win"):
                        pc = int(key[3:])
                        gp = 5 + pc
                        eng.wait_ge(S["sCW"], gp - 1)
                        eng.dma_start(out=stage[gp % 2][:],
                                      in_=win_t[:, pc * 2048:(pc + 1) * 2048]
                                      ).then_inc(S[f"qSG{gp % 2}"], 16)
                    elif key.startswith("x"):
                        c = int(key[1:])
                        if c >= 1:
                            eng.wait_ge(S["vCA"], c)
                        eng.dma_start(out=xch[:], in_=xc[c]).then_inc(S["qXC"], 16)
                    elif key.startswith("wp"):
                        i_, mt, pc = (int(v) for v in key[2:].split("_"))
                        gp = 13 + (i_ * 2 + mt) * 4 + pc
                        eng.wait_ge(S["sCW"], gp - 1)
                        wsrc = w1_t if mt == 0 else w2_t
                        eng.dma_start(out=stage[gp % 2][:],
                                      in_=wsrc[i_, :, pc * 2048:(pc + 1) * 2048]
                                      ).then_inc(S[f"qSG{gp % 2}"], 16)
                eng.sem_inc(S["pB"], 1)

            @blk0.scalar
            def _(eng):
                abf = ablk[:].rearrange("p a b -> p (a b)")
                bpbf = bpb[:].rearrange("p a b -> p (a b)")
                for gp in range(5):
                    eng.wait_ge(S[f"qSG{gp % 2}"], 16 * (gp // 2 + 1))
                    dsts = [abf[:, 0:2048], abf[:, 2048:4096],
                            bpbf[:, 0:2048], bpbf[:, 2048:4096],
                            gblk[:].rearrange("p a b -> p (a b)")]
                    srcsz = 2048 if gp < 4 else 256
                    eng.copy(dsts[gp], stage[gp % 2][:, 0:srcsz]).then_inc(S["sCW"], 1)
                for pc in range(8):
                    gp = 5 + pc
                    eng.wait_ge(S[f"qSG{gp % 2}"], 16 * (gp // 2 + 1))
                    eng.copy(slot0[pc // 4][:, (pc % 4) * 2048:(pc % 4 + 1) * 2048],
                             stage[gp % 2][:]).then_inc(S["sCW"], 1)
                wq = 0
                for c in range(NCH0):
                    eng.wait_ge(S["vCA"], c + 1)
                    for gq in range(4):
                        if 4 * c + gq >= 2:
                            eng.wait_ge(S["tSQG"], 4 * c + gq - 1)
                        eng.activation(out=xsqp[gq % 2][:],
                                       in_=xb[:, gq * 8:(gq + 1) * 8, :],
                                       func=AF.Square).then_inc(S["sSQ"], 1)
                    eng.wait_ge(S["vRW0"], c + 1)
                    eng.activation(out=rows[0:1, 1536:1536 + TC0],
                                   in_=rows[0:1, 1024:1024 + TC0],
                                   func=AF.Sqrt).then_inc(S["sIV0"], 1)
                    for _ in range(4):
                        if wq < len(wpieces):
                            i_, mt, pc = wpieces[wq]
                            gp = 13 + wq
                            eng.wait_ge(S[f"qSG{gp % 2}"], 16 * (gp // 2 + 1))
                            if wq >= 4:
                                eng.wait_ge(S["qWSP"], 16 * (wq - 3))
                            eng.copy(castb[wq % 4][:],
                                     stage[gp % 2][:]).then_inc(S["sCW"], 1)
                            wq += 1
                    eng.wait_ge(S["vRTa"], c + 1)
                    for j in range(NJ):
                        eng.activation(out=expc[:, j, :], in_=rts[:, j, :],
                                       func=AF.Exp, bias=nm1c[:, j:j + 1])
                    for j in range(NJ):
                        ins = eng.activation(out=rcpc[:, j:j + 1],
                                             in_=mdc[:, j:j + 1], func=AF.Sigmoid)
                    ins.then_inc(S["sRTb"], 1)
                eng.sem_inc(S["pB"], 1)

            @blk0.vector
            def _(eng):
                eng.wait_ge(S["qCON"], 16 * 10)
                eng.memset(onesb[:], 1.0)
                eng.memset(ones1[:], 1.0)
                eng.memset(on8[:], 1.0)
                eng.memset(zr8[:], 0.0)
                eng.memset(rscr[:], 0.0)
                eng.memset(eqc[:], 0.0)
                eng.memset(t8[:], 0.0)
                eng.memset(t9[:], 0.0)
                eng.memset(m1c[:], 0.0)
                eng.memset(m2c[:], 0.0)
                eng.tensor_add(bbs[:], bos[:], bbs[:]).then_inc(S["vTIN"], 1)

                def rpad():
                    eng.scalar_tensor_tensor(rscr[:], on8[:], 1.0, zr8[:],
                                             ALU.mult, ALU.add)
                    eng.scalar_tensor_tensor(rscr[:], on8[:], 1.0, zr8[:],
                                             ALU.mult, ALU.add)

                eng.wait_ge(S["qXC"], 16)
                eng.tensor_copy(xb[:], xch[:]).then_inc(S["vCA"], 1)
                for c in range(NCH0):
                    eng.wait_ge(S["tSQG"], 4 * (c + 1))
                    eng.tensor_scalar_mul(rows[0:1, 0:TC0], ps_rows[0:1, 0:TC0],
                                          -1.0 / D)
                    eng.tensor_mul(rows[0:1, 512:512 + TC0], rows[0:1, 0:TC0],
                                   rows[0:1, 0:TC0])
                    eng.scalar_tensor_tensor(rows[0:1, 1024:1024 + TC0],
                                             ps_rows[0:1, 256:256 + TC0],
                                             1.0 / D, rows[0:1, 512:512 + TC0],
                                             ALU.mult, ALU.subtract)
                    eng.tensor_scalar_add(rows[0:1, 512:512 + TC0],
                                          rows[0:1, 1024:1024 + TC0], EPS)
                    eng.reciprocal(rows[0:1, 1024:1024 + TC0],
                                   rows[0:1, 512:512 + TC0]).then_inc(S["vRW0"], 1)
                    eng.wait_ge(S["sIV0"], c + 1)
                    eng.tensor_mul(rows[0:1, 1792:1792 + TC0], rows[0:1, 0:TC0],
                                   rows[0:1, 1536:1536 + TC0]).then_inc(S["vMI0"], 1)
                    if c >= 1:
                        eng.wait_ge(S["sRTb"], c)
                        for j in range(NJ):
                            eng.tensor_scalar_sub(eqc[:, j, :], rts[:, j, :],
                                                  m2c[:, j:j + 1])
                        rpad()
                        for j in range(NJ):
                            eng.scalar_tensor_tensor(t8[:, j, :], eqc[:, j, :],
                                                     1e30, on8[:], ALU.mult,
                                                     ALU.add)
                        rpad()
                        for j in range(NJ):
                            eng.tensor_max(t9[:, j, :], t8[:, j, :], zr8[:])
                        rpad()
                        for j in range(NJ):
                            eng.tensor_tensor(eqc[:, j, :], t9[:, j, :], on8[:],
                                              ALU.min)
                        rpad()
                        for j in range(NJ):
                            ins = eng.scalar_tensor_tensor(rts[:, j, :],
                                                           expc[:, j, :],
                                                           rcpc[:, j:j + 1],
                                                           eqc[:, j, :],
                                                           ALU.mult, ALU.mult)
                        ins.then_inc(S["vRTc"], 1)
                        eng.wait_ge(S["tTR2"], c)
                        eng.tensor_copy(rwf[0:8, :],
                                        ps_mm[0:8, 512:512 + TC0]).then_inc(S["vRWF"], 1)
                        eng.wait_ge(S["tRWX"], c)
                        eng.tensor_mul(hw[:, (c - 1) * TC0:c * TC0], hAc[:],
                                       ps_mm[:, 256:512]).then_inc(S["vHW"], 1)
                    eng.wait_ge(S["tBC0"], c + 1)
                    if c >= 1:
                        eng.wait_ge(S["qSPILL"], 16 * c)
                    bc_i = ps_bc[:, 0:TC0].unsqueeze(1).broadcast_to([128, DKB, TC0])
                    bc_m = ps_bc[:, 256:256 + TC0].unsqueeze(1).broadcast_to([128, DKB, TC0])
                    eng.tensor_mul(xnb[:], bc_i, xb[:])
                    eng.tensor_tensor(xnb[:], bc_m, xnb[:],
                                      ALU.add).then_inc(S["vXN"], 1)
                    if c + 1 < NCH0:
                        eng.wait_ge(S["qXC"], 16 * (c + 2))
                        eng.wait_ge(S["tSQG"], 4 * (c + 1))
                        eng.wait_ge(S["sSQ"], 4 * (c + 1))
                        eng.tensor_copy(xb[:], xch[:]).then_inc(S["vCA"], 1)
                    eng.wait_ge(S["tPJ"], c + 1)
                    for mb in range(DB):
                        eng.tensor_scalar(h[:, mb, c * TC0:(c + 1) * TC0],
                                          ps_mm[:, mb * 512:mb * 512 + TC0],
                                          bins[:, mb:mb + 1], None, ALU.add)
                    eng.tensor_copy(hAc[:], ps_bc[:, 512:512 + TC0])
                    eng.tensor_copy(lgs[0:8, :],
                                    ps_bc[0:8, 768:768 + TC0]).then_inc(S["vEV0"], 1)
                    eng.wait_ge(S["tTR1"], c + 1)
                    for j in range(NJ):
                        eng.tensor_copy(rts[:, j, :], ps_mm[:, j * 8:(j + 1) * 8])
                    rpad()
                    for j in range(NJ):
                        eng.reduce_max(m1c[:, j:j + 1], rts[:, j, :], axis=AXX)
                    rpad()
                    for j in range(NJ):
                        eng.tensor_scalar_mul(nm1c[:, j:j + 1], m1c[:, j:j + 1],
                                              -1.0)
                    rpad()
                    for j in range(NJ):
                        eng.tensor_scalar_sub(eqc[:, j, :], rts[:, j, :],
                                              m1c[:, j:j + 1])
                    rpad()
                    for j in range(NJ):
                        eng.scalar_tensor_tensor(t8[:, j, :], eqc[:, j, :], 1e30,
                                                 on8[:], ALU.mult, ALU.add)
                    rpad()
                    for j in range(NJ):
                        eng.tensor_max(t9[:, j, :], t8[:, j, :], zr8[:])
                    rpad()
                    for j in range(NJ):
                        eng.tensor_tensor(eqc[:, j, :], t9[:, j, :], on8[:],
                                          ALU.min)
                    rpad()
                    for j in range(NJ):
                        eng.scalar_tensor_tensor(t8[:, j, :], eqc[:, j, :],
                                                 -1e30, rts[:, j, :],
                                                 ALU.mult, ALU.add)
                    rpad()
                    for j in range(NJ):
                        eng.reduce_max(m2c[:, j:j + 1], t8[:, j, :], axis=AXX)
                    rpad()
                    for j in range(NJ):
                        ins = eng.tensor_tensor(mdc[:, j:j + 1], m1c[:, j:j + 1],
                                                m2c[:, j:j + 1], ALU.subtract)
                    ins.then_inc(S["vRTa"], 1)
                c = NCH0
                eng.wait_ge(S["sRTb"], c)
                for j in range(NJ):
                    eng.tensor_scalar_sub(eqc[:, j, :], rts[:, j, :],
                                          m2c[:, j:j + 1])
                rpad()
                for j in range(NJ):
                    eng.scalar_tensor_tensor(t8[:, j, :], eqc[:, j, :],
                                             1e30, on8[:], ALU.mult,
                                             ALU.add)
                rpad()
                for j in range(NJ):
                    eng.tensor_max(t9[:, j, :], t8[:, j, :], zr8[:])
                rpad()
                for j in range(NJ):
                    eng.tensor_tensor(eqc[:, j, :], t9[:, j, :], on8[:],
                                      ALU.min)
                rpad()
                for j in range(NJ):
                    ins = eng.scalar_tensor_tensor(rts[:, j, :],
                                                   expc[:, j, :],
                                                   rcpc[:, j:j + 1],
                                                   eqc[:, j, :],
                                                   ALU.mult, ALU.mult)
                ins.then_inc(S["vRTc"], 1)
                eng.wait_ge(S["tTR2"], c)
                eng.tensor_copy(rwf[0:8, :],
                                ps_mm[0:8, 512:512 + TC0]).then_inc(S["vRWF"], 1)
                eng.wait_ge(S["tRWX"], c)
                eng.tensor_mul(hw[:, (c - 1) * TC0:c * TC0], hAc[:],
                               ps_mm[:, 256:512]).then_inc(S["vHW"], 1)
                eng.sem_inc(S["pB"], 1)

            @blk0.tensor
            def _(eng):
                eng.wait_ge(S["vCA"], 1)
                for kb in range(DKB):
                    eng.matmul(ps_rows[0:1, 0:TC0], onesb[:], xb[:, kb, :],
                               start=(kb == 0), stop=(kb == DKB - 1))
                for gq in range(4):
                    eng.wait_ge(S["sSQ"], gq + 1)
                    for kk in range(8):
                        mm = eng.matmul(ps_rows[0:1, 256:256 + TC0], onesb[:],
                                        xsqp[gq % 2][:, kk, :],
                                        start=(gq == 0 and kk == 0),
                                        stop=(gq == 3 and kk == 7))
                    mm.then_inc(S["tSQG"], 1)
                for c in range(NCH0):
                    if c > 0:
                        eng.wait_ge(S["vRTc"], c)
                        eng.wait_ge(S["vEV0"], c)
                        for j in range(NJ):
                            mm = eng.transpose(ps_mm[0:8, 512 + j * 128:512 + (j + 1) * 128],
                                               rts[:, j, :], ident[:])
                        mm.then_inc(S["tTR2"], 1)
                        eng.wait_ge(S["vRWF"], c)
                        if c > 1:
                            eng.wait_ge(S["vHW"], c - 1)
                        eng.matmul(ps_mm[:, 256:512], sel8s[0:8, :], rwf[0:8, 0:TC0],
                                   start=True, stop=True).then_inc(S["tRWX"], 1)
                    eng.wait_ge(S["vMI0"], c + 1)
                    if c > 0:
                        eng.wait_ge(S["vXN"], c)
                    eng.matmul(ps_bc[:, 0:512], ones1[:], rows[0:1, 1536:2048],
                               start=True, stop=True).then_inc(S["tBC0"], 1)
                    eng.wait_ge(S["vXN"], c + 1)
                    eng.wait_ge(S["sCW"], 13)
                    if c > 0:
                        eng.wait_ge(S["vEV0"], c)
                        eng.wait_ge(S["vRTa"], c)
                    for mb in range(DB):
                        for kb in range(DKB):
                            eng.matmul(ps_mm[:, mb * 512:mb * 512 + TC0],
                                       win_ap(mb, kb), xnb[:, kb, :],
                                       start=(kb == 0), stop=(kb == DKB - 1))
                    for kb in range(DKB):
                        eng.matmul(ps_bc[:, 512:512 + TC0], ablk[:, kb, :],
                                   xnb[:, kb, :], start=(kb == 0),
                                   stop=(kb == DKB - 1))
                    for kb in range(DKB):
                        mm = eng.matmul(ps_bc[0:8, 768:768 + TC0], gblk[:, kb, :],
                                        xnb[:, kb, :],
                                        start=(kb == 0), stop=(kb == DKB - 1))
                    mm.then_inc(S["tPJ"], 1)
                    eng.wait_ge(S["vEV0"], c + 1)
                    for j in range(NJ):
                        mm = eng.transpose(ps_mm[:, j * 8:(j + 1) * 8],
                                           lgs[0:8, j * 128:(j + 1) * 128],
                                           ident[0:8, 0:8])
                    mm.then_inc(S["tTR1"], 1)
                    if c + 1 < NCH0:
                        eng.wait_ge(S["vCA"], c + 2)
                        eng.wait_ge(S["vRW0"], c + 1)
                        for kb in range(DKB):
                            eng.matmul(ps_rows[0:1, 0:TC0], onesb[:], xb[:, kb, :],
                                       start=(kb == 0), stop=(kb == DKB - 1))
                        for gq in range(4):
                            eng.wait_ge(S["sSQ"], (c + 1) * 4 + gq + 1)
                            for kk in range(8):
                                mm = eng.matmul(ps_rows[0:1, 256:256 + TC0],
                                                onesb[:], xsqp[gq % 2][:, kk, :],
                                                start=(gq == 0 and kk == 0),
                                                stop=(gq == 3 and kk == 7))
                            mm.then_inc(S["tSQG"], 1)
                eng.wait_ge(S["vRTc"], NCH0)
                for j in range(NJ):
                    mm = eng.transpose(ps_mm[0:8, 512 + j * 128:512 + (j + 1) * 128],
                                       rts[:, j, :], ident[:])
                mm.then_inc(S["tTR2"], 1)
                eng.wait_ge(S["vRWF"], NCH0)
                eng.wait_ge(S["vHW"], NCH0 - 1)
                eng.matmul(ps_mm[:, 256:512], sel8s[0:8, :], rwf[0:8, 0:TC0],
                           start=True, stop=True).then_inc(S["tRWX"], 1)
                eng.sem_inc(S["pB"], 1)

            @blk0.gpsimd
            def _(eng):
                def spill(wq):
                    i_, mt, pc = wpieces[wq]
                    eng.wait_ge(S["sCW"], 13 + wq + 1)
                    eng.dma_start(out=w_scr[i_, mt, :, pc * 2048:(pc + 1) * 2048],
                                  in_=castb[wq % 4][:]).then_inc(S["qWSP"], 16)
                for c in range(NCH0):
                    if c >= 1:
                        for wq in range(4 * (c - 1), 4 * c):
                            spill(wq)
                    eng.wait_ge(S["vXN"], c + 1)
                    eng.dma_start(out=xn_scr[c], in_=xnb[:]).then_inc(S["qSPILL"], 16)
                for wq in range(4 * (NCH0 - 1), 32):
                    spill(wq)
                eng.wait_ge(S["qSPILL"], 16 * NCH0)
                eng.wait_ge(S["qWSP"], 16 * 32)
                eng.sem_inc(S["pB"], 1)

        if STAGE < 2:
            return nc

        es1 = contextlib.ExitStack()
        with es1:
            E1 = es1.enter_context
            slot = [E1(nc.sbuf_tensor(f"w1slot{j}", [128, 8192], BF16)) for j in range(3)]
            C0 = E1(nc.sbuf_tensor("C0", [128, DB, TC1], BF16))
            S0 = E1(nc.sbuf_tensor("S0", [128, DB, TC1], BF16))
            RC0 = E1(nc.sbuf_tensor("RC0", [128, DB, TC1], BF16))
            RS0 = E1(nc.sbuf_tensor("RS0", [128, DB, TC1], BF16))
            yb = E1(nc.sbuf_tensor("yb", [128, DB, TC1], BF16))
            ub = E1(nc.sbuf_tensor("ub", [128, DB, TC1], BF16))
            vb = yb
            Ub = E1(nc.sbuf_tensor("Ub", [128, DB, TC1], BF16))
            Vb = E1(nc.sbuf_tensor("Vb", [128, DB, TC1], BF16))
            cyb = E1(nc.sbuf_tensor("cyb", [128, DB, TC1], BF16))
            syb = E1(nc.sbuf_tensor("syb", [128, DB, TC1], BF16))
            qa = E1(nc.sbuf_tensor("qa", [128, DB, TC1], F32))
            hb = E1(nc.sbuf_tensor("hb", [128, DB, TC1], BF16))
            hqb = E1(nc.sbuf_tensor("hqb", [128, DB, TC1], BF16))
            xh = E1(nc.sbuf_tensor("xh", [128, DB, TC1], BF16))
            stgCf = E1(nc.sbuf_tensor("stgCf", [128, DB, TC1], F32))
            stgC = stgCf
            stgS = qa
            q6 = E1(nc.sbuf_tensor("q6", [128, DB, TC1], BF16))
            zb = E1(nc.sbuf_tensor("zb", [128, FB, TC1], BF16))
            y2b = [E1(nc.sbuf_tensor(f"y2b{j}", [128, DB, TC1], BF16))
                   for j in range(3)]

            def w1ap(i, kb, mi):
                o = (kb * FB + mi) * 128
                return slot[w1slot(i)][:, o:o + 128]

            def w2ap(fb, mb):
                o = (fb * DB + mb) * 128
                return slot[1][:, o:o + 128]

            blk1 = E1(nc.Block("p1"))

            def p1_sync(eng):
                eng.wait_ge(S["pB"], 5)
                eng.wait_ge(S["qWSP"], 16 * 32)
                for i in range(NL):
                    if i >= 2:
                        eng.wait_ge(S["tM1T"], 64 * (i - 1) + 16)
                    eng.dma_start(out=slot[w1slot(i)][:],
                                  in_=w_scr[i, 0]).then_inc(S["qWL"], 16)
                    if i >= 1:
                        eng.wait_ge(S["tM2T"], 16 * i)
                    eng.dma_start(out=slot[1][:],
                                  in_=w_scr[i, 1]).then_inc(S["qWL"], 16)
                eng.sem_inc(S["pB"], 1)

            def p1_vector(eng):
                eng.wait_ge(S["pB"], 5)
                colr2 = tcol("r2", DB)
                colrho = tcol("rho", DB)
                colrinv = tcol("rinv", DB)
                colc0 = tcol("c0", DB)
                cols0 = tcol("s0", DB)
                colt0 = tcol("t0", DB)
                colt1 = tcol("Ulast", DB)
                colt3 = tcol("Vlast", DB)
                colt2 = tcol("t2", DB)
                colab = tcol("ab", 8)
                colg1 = tcol("g1", 8)
                colg2 = tcol("g2", 8)
                colg3 = tcol("g3", 8)
                colZ = tcol("Z", 8)
                colw = tcol("w", 8)
                colzw = tcol("zw", 8)
                colEa = tcol("Ea", 8)
                colTk = tcol("Tk", 8)
                colx0 = tcol("x0", DB)
                colx1 = tcol("x1", DB)
                colcm2 = tcol("cm2", DB)
                colcm3 = tcol("cm3", DB)
                colTk1 = tcol("Tk1", 8)
                colTk2 = tcol("Tk2", 8)
                colc1 = tcol("c1", 8)
                colc2v = tcol("c2", 8)
                def v_ph(i):
                    if i > 0:
                        eng.wait_ge(S["sPH"], i)
                    for b in range(DB):
                        eng.tensor_mul(colx0[:, b:b + 1], pcol(i, P_PRE, b),
                                       pcol(i, P_PRE, b))
                    for b in range(DB):
                        eng.tensor_mul(colx1[:, b:b + 1], pcol(i, P_PIM, b),
                                       pcol(i, P_PIM, b))
                    for b in range(DB):
                        ins = eng.tensor_add(colr2[:, b:b + 1], colx0[:, b:b + 1],
                                             colx1[:, b:b + 1])
                    ins.then_inc(S["vPH"], 1)
                    eng.wait_ge(S["sPH"], i + 1)
                    eng.reciprocal(colrinv, tcol("rr", DB))
                    eng.tensor_copy(rtabv(0), colrho)
                    for b in range(DB):
                        eng.tensor_mul(colc0[:, b:b + 1], pcol(i, P_PRE, b),
                                       colrinv[:, b:b + 1])
                    for b in range(DB):
                        eng.tensor_mul(cols0[:, b:b + 1], pcol(i, P_PIM, b),
                                       colrinv[:, b:b + 1])
                    eng.tensor_copy(tabv(0, 0), colc0)
                    eng.tensor_copy(tabv(0, 1), cols0)
                    eng.memset(stgC[:, :, 0:1], 1.0)
                    eng.memset(stgS[:, :, 0:1], 0.0)
                    for j in range(12):
                        eng.tensor_mul(colx0, tabv(j, 0), tabv(j, 0))
                        eng.tensor_mul(colx1, tabv(j, 1), tabv(j, 1))
                        eng.tensor_mul(colt2, tabv(j, 0), tabv(j, 1))
                        eng.tensor_mul(rtabv(j + 1), rtabv(j), rtabv(j))
                        eng.tensor_tensor(tabv(j + 1, 0), colx0, colx1, ALU.subtract)
                        eng.tensor_add(tabv(j + 1, 1), colt2, colt2)
                    eng.memset(colab[:, 0:4], 1.0)
                    eng.memset(colab[:, 4:8], 0.0)
                    eng.tensor_mul(colg1[:, 0:4], rtabv(11), tabv(11, 0))
                    eng.tensor_mul(colg1[:, 4:8], rtabv(11), tabv(11, 1))
                    emit_cplx_mul(eng, colg2, colg1, colg1, colx0, colx1,
                                  colcm2, colcm3)
                    emit_cplx_mul(eng, colg3, colg2, colg1, colx0, colx1,
                                  colcm2, colcm3)
                def v_bases(i):
                    if i > 0:
                        eng.wait_ge(S["gCMB"], NCH1 * i)
                    for basekind in range(2):
                        if basekind == 1:
                            eng.memset(stgC[:, :, 0:1], 1.0)
                            eng.memset(stgS[:, :, 0:1], 0.0)
                        n, j = 1, 0
                        while n < TC1:
                            if basekind == 0:
                                fc = [tabc(j, b, 0) for b in range(DB)]
                                fs = [tabc(j, b, 1) for b in range(DB)]
                            else:
                                for b in range(DB):
                                    eng.tensor_mul(colx0[:, b:b + 1], rtabc(j, b),
                                                   tabc(j, b, 0))
                                for b in range(DB):
                                    eng.tensor_mul(colx1[:, b:b + 1], rtabc(j, b),
                                                   tabc(j, b, 1))
                                fc = [colx0[:, b:b + 1] for b in range(DB)]
                                fs = [colx1[:, b:b + 1] for b in range(DB)]
                            for b in range(DB):
                                eng.tensor_scalar_mul(stgC[:, b, n:2 * n],
                                                      stgS[:, b, 0:n], fs[b])
                            for b in range(DB):
                                eng.scalar_tensor_tensor(stgC[:, b, n:2 * n],
                                                         stgC[:, b, 0:n], fc[b],
                                                         stgC[:, b, n:2 * n],
                                                         ALU.mult, ALU.subtract)
                            for b in range(DB):
                                eng.tensor_scalar_mul(stgS[:, b, n:2 * n],
                                                      stgC[:, b, 0:n], fs[b])
                            for b in range(DB):
                                eng.scalar_tensor_tensor(stgS[:, b, n:2 * n],
                                                         stgS[:, b, 0:n], fc[b],
                                                         stgS[:, b, n:2 * n],
                                                         ALU.mult, ALU.add)
                            n *= 2
                            j += 1
                        if basekind == 0:
                            eng.tensor_copy(C0[:], stgC[:])
                            eng.tensor_copy(S0[:], stgS[:])
                        else:
                            eng.tensor_copy(RC0[:], stgC[:])
                            eng.tensor_scalar_mul(RS0[:], stgS[:], -1.0).then_inc(S["gBAS"], 1)
                def v_sconv(i, k):
                    idx = NCH1 * i + k
                    G1 = 8 * i + k
                    tok = slice(k * TC1, (k + 1) * TC1)

                    def ep1(r_, e_, i=i):
                        pass
                    emit_ln("vector", eng, G1, 0, h[:, :, tok], ep1, part="rows1")
                    if k < NCH1 - 1:
                        tok2 = slice((k + 1) * TC1, (k + 2) * TC1)
                        emit_ln("vector", eng, G1 + 1, 0, h[:, :, tok2], ep1,
                                part="copy")
                    emit_ln("vector", eng, G1, 0, h[:, :, tok], ep1, part="rows2")
                    emit_ln("vector", eng, G1, 0, h[:, :, tok], ep1, part="norm")
                    eng.wait_ge(S["sY1"], 2 * idx + 1)
                    eng.tensor_mul(cyb[:, 0:2, :], C0[:, 0:2, :], yb[:, 0:2, :])
                    eng.tensor_mul(syb[:, 0:2, :], S0[:, 0:2, :], yb[:, 0:2, :])
                    for b in range(2):
                        eng.tensor_scalar_mul(ub[:, b, :], syb[:, b, :],
                                              colab[:, 4 + b:4 + b + 1])
                    eng.wait_ge(S["sY1"], 2 * (idx + 1))
                    eng.tensor_mul(cyb[:, 2:4, :], C0[:, 2:4, :], yb[:, 2:4, :])
                    eng.tensor_mul(syb[:, 2:4, :], S0[:, 2:4, :], yb[:, 2:4, :])
                    for b in range(2, DB):
                        eng.tensor_scalar_mul(ub[:, b, :], syb[:, b, :],
                                              colab[:, 4 + b:4 + b + 1])
                    for b in range(DB):
                        eng.scalar_tensor_tensor(ub[:, b, :], cyb[:, b, :],
                                                 colab[:, b:b + 1], ub[:, b, :],
                                                 ALU.mult, ALU.subtract)
                    for b in range(DB):
                        eng.tensor_scalar_mul(vb[:, b, :], cyb[:, b, :],
                                              colab[:, 4 + b:4 + b + 1])
                    for b in range(DB):
                        ins = eng.scalar_tensor_tensor(vb[:, b, :], syb[:, b, :],
                                                       colab[:, b:b + 1],
                                                       vb[:, b, :], ALU.mult,
                                                       ALU.add)
                    ins.then_inc(S["vPREP"], 1)
                    for b in range(DB):
                        init = 0.0 if k == 0 else colt1[:, b:b + 1]
                        ins = eng.tensor_tensor_scan(
                            Ub[:, b, :],
                            colrho[:, b:b + 1].broadcast_to([128, TC1]),
                            ub[:, b, :], init, ALU.mult, ALU.add)
                    ins.then_inc(S["vSCU"], 1)
                    for b in range(DB):
                        initv = 0.0 if k == 0 else colt3[:, b:b + 1]
                        eng.tensor_tensor_scan(
                            Vb[:, b, :],
                            colrho[:, b:b + 1].broadcast_to([128, TC1]),
                            vb[:, b, :], initv, ALU.mult, ALU.add)
                    for b in range(DB):
                        eng.tensor_copy(colt1[:, b:b + 1], Ub[:, b, TC1 - 1:TC1])
                    for b in range(DB):
                        ins = eng.tensor_copy(colt3[:, b:b + 1],
                                              Vb[:, b, TC1 - 1:TC1])
                    ins.then_inc(S["vSCAN"], 1)
                    eng.tensor_mul(syb[:], C0[:], Ub[:])
                    eng.tensor_mul(cyb[:], S0[:], Vb[:])
                    eng.tensor_tensor(syb[:], cyb[:], syb[:], ALU.add)
                    eng.wait_ge(S["gCMB"], idx + 1)
                    for b in range(DB):
                        eng.tensor_scalar_mul(ub[:, b, :], q6[:, b, :],
                                              colab[:, 4 + b:4 + b + 1])
                    for b in range(DB):
                        eng.scalar_tensor_tensor(ub[:, b, :], syb[:, b, :],
                                                 colab[:, b:b + 1], ub[:, b, :],
                                                 ALU.mult, ALU.add)
                    for b in range(DB):
                        ins = eng.scalar_tensor_tensor(h[:, b, tok], ub[:, b, :],
                                                       pcol(i, P_SCL, b),
                                                       h[:, b, tok],
                                                       ALU.mult, ALU.add)
                    ins.then_inc(S["vH1"], 1)
                    if k < NCH1 - 1:
                        zz = tcol("zz", 8)
                        eng.tensor_mul(colx0, colab[:, 0:4], tabv(9, 1))
                        eng.tensor_mul(colx1, colab[:, 4:8], tabv(9, 1))
                        eng.tensor_mul(zz[:, 0:4], colab[:, 0:4], tabv(9, 0))
                        eng.tensor_mul(zz[:, 4:8], colab[:, 4:8], tabv(9, 0))
                        eng.tensor_tensor(colab[:, 0:4], zz[:, 0:4],
                                          colx1, ALU.subtract)
                        eng.tensor_tensor(colab[:, 4:8], zz[:, 4:8],
                                          colx0, ALU.add)

                def v_chain(i):
                    eC = tcol("eC", DB)
                    eS_ = tcol("eS", DB)
                    eng.tensor_mul(colx0, tabv(11, 0), tabv(0, 0))
                    eng.tensor_mul(colx1, tabv(11, 1), tabv(0, 1))
                    eng.tensor_mul(colcm2, tabv(11, 1), tabv(0, 0))
                    eng.tensor_mul(colcm3, tabv(11, 0), tabv(0, 1))
                    eng.tensor_add(eC, colx0, colx1)
                    eng.tensor_tensor(eS_, colcm2, colcm3, ALU.subtract)
                    eng.tensor_mul(colx0, eC, colt1)
                    eng.tensor_mul(colx1, eS_, colt3)
                    eng.tensor_mul(colcm2, eS_, colt1)
                    eng.tensor_mul(colcm3, eC, colt3)
                    eng.tensor_add(colEa[:, 0:4], colx0, colx1)
                    eng.tensor_tensor(colEa[:, 4:8], colcm2, colcm3, ALU.subtract)
                    for b in range(DB):
                        eng.tensor_copy(ccs[:, 2 * b:2 * b + 1], colEa[:, b:b + 1])
                    for b in range(DB):
                        ins = eng.tensor_copy(ccs[:, 2 * b + 1:2 * b + 2],
                                              colEa[:, 4 + b:4 + b + 1])
                    ins.then_inc(S["vE"], 1)
                    eng.wait_ge(S["cCC"], i + 1)
                    eng.wait_ge(S["qGTH"], 16 * (i + 1))
                    TKB = [colTk, colTk1, colTk2]
                    for s_ in range(8):
                        for kk in range(3):
                            msk = mks[:, 5 + s_ * 3 + kk:5 + s_ * 3 + kk + 1]
                            if s_ == 0:
                                eng.tensor_scalar_mul(TKB[kk][:], gth[:, s_, :], msk)
                            else:
                                eng.scalar_tensor_tensor(TKB[kk][:], gth[:, s_, :],
                                                         msk, TKB[kk][:],
                                                         ALU.mult, ALU.add)
                    for g, Tsrc, cdst in ((colg1, colTk1, colc1),
                                          (colg2, colTk2, colc2v)):
                        for b in range(DB):
                            eng.tensor_mul(colx0[:, b:b + 1], g[:, b:b + 1],
                                           Tsrc[:, 2 * b:2 * b + 1])
                        for b in range(DB):
                            eng.tensor_mul(colx1[:, b:b + 1], g[:, 4 + b:4 + b + 1],
                                           Tsrc[:, 2 * b + 1:2 * b + 2])
                        for b in range(DB):
                            eng.tensor_mul(colcm2[:, b:b + 1], g[:, b:b + 1],
                                           Tsrc[:, 2 * b + 1:2 * b + 2])
                        for b in range(DB):
                            eng.tensor_mul(colcm3[:, b:b + 1], g[:, 4 + b:4 + b + 1],
                                           Tsrc[:, 2 * b:2 * b + 1])
                        for b in range(DB):
                            eng.tensor_tensor(cdst[:, b:b + 1], colx0[:, b:b + 1],
                                              colx1[:, b:b + 1], ALU.subtract)
                        for b in range(DB):
                            eng.tensor_add(cdst[:, 4 + b:4 + b + 1],
                                           colcm2[:, b:b + 1], colcm3[:, b:b + 1])
                    for b in range(DB):
                        eng.tensor_add(colZ[:, b:b + 1], colTk[:, 2 * b:2 * b + 1],
                                       colc1[:, b:b + 1])
                    for b in range(DB):
                        eng.tensor_add(colZ[:, 4 + b:4 + b + 1],
                                       colTk[:, 2 * b + 1:2 * b + 2],
                                       colc1[:, 4 + b:4 + b + 1])
                    for b in range(DB):
                        eng.tensor_add(colZ[:, b:b + 1], colc2v[:, b:b + 1],
                                       colZ[:, b:b + 1])
                    for b in range(DB):
                        eng.tensor_add(colZ[:, 4 + b:4 + b + 1],
                                       colc2v[:, 4 + b:4 + b + 1],
                                       colZ[:, 4 + b:4 + b + 1])
                    glr = tcol("glr", DB)
                    gli = tcol("gli", DB)
                    eng.tensor_scalar(glr, mks[:, 1:2].broadcast_to([128, DB]), 1.0,
                                      None, ALU.mult)
                    eng.tensor_scalar(gli, mks[:, 1:2].broadcast_to([128, DB]), 0.0,
                                      None, ALU.mult)
                    for kk, g in [(1, colg1), (2, colg2), (3, colg3)]:
                        sel = mks[:, 1 + kk:2 + kk]
                        for b in range(DB):
                            eng.scalar_tensor_tensor(glr[:, b:b + 1], g[:, b:b + 1],
                                                     sel, glr[:, b:b + 1],
                                                     ALU.mult, ALU.add)
                        for b in range(DB):
                            eng.scalar_tensor_tensor(gli[:, b:b + 1],
                                                     g[:, 4 + b:4 + b + 1], sel,
                                                     gli[:, b:b + 1], ALU.mult,
                                                     ALU.add)
                    for b in range(DB):
                        eng.tensor_mul(colx0[:, b:b + 1], glr[:, b:b + 1],
                                       pcol(i, P_LCR, b))
                    for b in range(DB):
                        eng.tensor_mul(colx1[:, b:b + 1], gli[:, b:b + 1],
                                       pcol(i, P_LCI, b))
                    for b in range(DB):
                        eng.tensor_mul(colcm2[:, b:b + 1], glr[:, b:b + 1],
                                       pcol(i, P_LCI, b))
                    for b in range(DB):
                        eng.tensor_mul(colcm3[:, b:b + 1], gli[:, b:b + 1],
                                       pcol(i, P_LCR, b))
                    for b in range(DB):
                        eng.tensor_tensor(colx0[:, b:b + 1], colx0[:, b:b + 1],
                                          colx1[:, b:b + 1], ALU.subtract)
                    for b in range(DB):
                        eng.tensor_add(colcm2[:, b:b + 1], colcm2[:, b:b + 1],
                                       colcm3[:, b:b + 1])
                    for b in range(DB):
                        eng.tensor_add(colZ[:, b:b + 1], colx0[:, b:b + 1],
                                       colZ[:, b:b + 1])
                    for b in range(DB):
                        eng.tensor_add(colZ[:, 4 + b:4 + b + 1], colcm2[:, b:b + 1],
                                       colZ[:, 4 + b:4 + b + 1])
                    pr = tcol("p_r", DB)
                    pim = tcol("p_i", DB)
                    eng.tensor_mul(pr, rtabv(0), tabv(0, 0))
                    eng.tensor_mul(pim, rtabv(0), tabv(0, 1))
                    for b in range(DB):
                        eng.tensor_mul(colx0[:, b:b + 1], pr[:, b:b + 1],
                                       colZ[:, b:b + 1])
                    for b in range(DB):
                        eng.tensor_mul(colx1[:, b:b + 1], pim[:, b:b + 1],
                                       colZ[:, 4 + b:4 + b + 1])
                    for b in range(DB):
                        eng.tensor_mul(colcm2[:, b:b + 1], pr[:, b:b + 1],
                                       colZ[:, 4 + b:4 + b + 1])
                    for b in range(DB):
                        eng.tensor_mul(colcm3[:, b:b + 1], pim[:, b:b + 1],
                                       colZ[:, b:b + 1])
                    for b in range(DB):
                        eng.tensor_tensor(colw[:, b:b + 1], colx0[:, b:b + 1],
                                          colx1[:, b:b + 1], ALU.subtract)
                    for b in range(DB):
                        eng.tensor_add(colw[:, 4 + b:4 + b + 1], colcm2[:, b:b + 1],
                                       colcm3[:, b:b + 1])
                    for b in range(DB):
                        eng.tensor_mul(colzw[:, b:b + 1], colw[:, b:b + 1],
                                       pcol(i, P_SCL, b))
                    for b in range(DB):
                        eng.tensor_mul(colzw[:, 4 + b:4 + b + 1],
                                       colw[:, 4 + b:4 + b + 1], pcol(i, P_SCL, b))
                    for k in range(NCH1):
                        tok = slice(k * TC1, (k + 1) * TC1)
                        for b in range(DB):
                            eng.scalar_tensor_tensor(h[:, b, tok], RC0[:, b, :],
                                                     colzw[:, b:b + 1], h[:, b, tok],
                                                     ALU.mult, ALU.add)
                            eng.scalar_tensor_tensor(h[:, b, tok], RS0[:, b, :],
                                                     colzw[:, 4 + b:4 + b + 1],
                                                     h[:, b, tok],
                                                     ALU.mult, ALU.add)
                        if k == 0:
                            emit_ln("vector", eng, 8 * i + 4, 2048,
                                    h[:, :, 0:TC1], lambda r_, e_: None,
                                    part="copy")
                        if k < NCH1 - 1:
                            zz = tcol("zz", 8)
                            eng.tensor_mul(colx0, colzw[:, 0:4], tabv(9, 1))
                            eng.tensor_mul(colx1, colzw[:, 4:8], tabv(9, 1))
                            eng.tensor_mul(zz[:, 0:4], colzw[:, 0:4], tabv(9, 0))
                            eng.tensor_mul(zz[:, 4:8], colzw[:, 4:8], tabv(9, 0))
                            eng.tensor_tensor(colx1, zz[:, 0:4], colx1, ALU.subtract)
                            eng.tensor_tensor(colx0, zz[:, 4:8], colx0, ALU.add)
                            eng.tensor_mul(colzw[:, 0:4], colx1, rtabv(9))
                            eng.tensor_mul(colzw[:, 4:8], colx0, rtabv(9))
                    eng.engine_nop().then_inc(S["vZ"], 1)
                    if DEBUG_TAPS and i == 0:
                        eng.wait_ge(S["qDBG"], 16 * 8)
                def v_ln2(i, k, part="full"):
                    idx = NCH1 * i + k
                    G2 = 8 * i + 4 + k
                    tok = slice(k * TC1, (k + 1) * TC1)

                    def ep2(r_, e_, i=i, idx=idx):
                        if r_ == "vector":
                            if idx >= 3:
                                e_.wait_ge(S["tM1T"], 16 * (idx - 2))
                            for b in range(DB):
                                ins2 = e_.tensor_scalar(y2b[idx % 3][:, b, :],
                                                        xh[:, b, :],
                                                        pcol(i, P_LN2W, b),
                                                        pcol(i, P_LN2B, b),
                                                        ALU.mult, ALU.add)
                            ins2.then_inc(S["vY2"], 1)
                    emit_ln("vector", eng, G2, 2048, h[:, :, tok], ep2, part=part)

                def v_hupd(i, k):
                    idx = NCH1 * i + k
                    tok = slice(k * TC1, (k + 1) * TC1)
                    eng.wait_ge(S["tM2T"], 4 * (idx + 1))
                    for mb in range(DB):
                        ins = eng.scalar_tensor_tensor(h[:, mb, tok],
                                                       ps_mm[:, mb * 512:(mb + 1) * 512],
                                                       pcol(i, P_B2, mb),
                                                       h[:, mb, tok],
                                                       ALU.add, ALU.add)
                    ins.then_inc(S["vHF"], 1)

                def v_lncopy(i, k):
                    G1 = 8 * i + k
                    tok = slice(k * TC1, (k + 1) * TC1)
                    emit_ln("vector", eng, G1, 0, h[:, :, tok],
                            lambda r_, e_: None, part="copy")

                def v_ln2batch(i):
                    v_ln2(i, 0, "rows")
                    v_ln2(i, 1, "copy")
                    v_ln2(i, 0, "norm")
                    v_ln2(i, 1, "rows")
                    v_ln2(i, 2, "copy")
                    v_ln2(i, 1, "norm")
                    v_ln2(i, 2, "rows")
                    v_ln2(i, 3, "copy")
                    v_ln2(i, 2, "norm")
                    v_ln2(i, 3, "rows")
                    v_ln2(i, 3, "norm")

                if SCHED == 0:
                    for i in range(NL):
                        v_ph(i)
                        v_bases(i)
                        for k in range(NCH1):
                            v_sconv(i, k)
                        v_chain(i)
                        for k in range(NCH1):
                            v_ln2(i, k)
                            v_hupd(i, k)
                else:
                    v_ph(0)
                    v_bases(0)
                    v_lncopy(0, 0)
                    for k in range(NCH1):
                        v_sconv(0, k)
                    v_chain(0)
                    for i in range(NL):
                        v_ln2batch(i)
                        if i + 1 < NL:
                            v_ph(i + 1)
                        v_hupd(i, 0)
                        v_hupd(i, 1)
                        if i + 1 < NL:
                            v_bases(i + 1)
                            v_lncopy(i + 1, 0)
                            v_sconv(i + 1, 0)
                        v_hupd(i, 2)
                        if i + 1 < NL:
                            v_sconv(i + 1, 1)
                        v_hupd(i, 3)
                        if i + 1 < NL:
                            v_sconv(i + 1, 2)
                            v_sconv(i + 1, 3)
                            v_chain(i + 1)
                eng.sem_inc(S["pB"], 1)

            def p1_scalar(eng):
                eng.wait_ge(S["pB"], 5)
                colr2 = tcol("r2", DB)
                colrr = tcol("rr", DB)
                colrho = tcol("rho", DB)
                colrinv = tcol("rinv", DB)

                def s_ph(i):
                    eng.wait_ge(S["vPH"], i + 1)
                    eng.activation(out=colrr, in_=colr2, func=AF.Sqrt)
                    eng.activation(out=colrinv, in_=colr2, func=AF.Square)
                    eng.activation(out=colrho, in_=colrr,
                                   func=AF.Tanh).then_inc(S["sPH"], 1)

                def s_g1(i, k, part="full"):
                    G1 = 8 * i + k
                    idx = NCH1 * i + k

                    def ep1(r_, e_, i=i, idx=idx, G1=G1):
                        if r_ == "scalar":
                            e_.wait_ge(S["vNRM"], G1 + 1)
                            if idx > 0:
                                e_.wait_ge(S["vSCAN"], idx)
                            for b in range(DB):
                                ins = e_.activation(out=yb[:, b, :],
                                                    in_=xh[:, b, :], func=AF.Silu,
                                                    bias=pcol(i, P_LN1B, b),
                                                    scale=pcol(i, P_LN1W, b))
                                if b == 1:
                                    ins.then_inc(S["sY1"], 1)
                            ins.then_inc(S["sY1"], 1)
                    emit_ln("scalar", eng, G1, 0, h[:, :, 0:1], ep1, part=part)

                def s_g2(i, k):
                    G2 = 8 * i + 4 + k

                    def ep2(r_, e_):
                        pass
                    emit_ln("scalar", eng, G2, 2048, h[:, :, 0:1], ep2)

                def s_zb(i, k):
                    idx = NCH1 * i + k
                    if idx > 0:
                        eng.wait_ge(S["tM2T"], 4 * idx)
                    for mi in range(FB):
                        eng.wait_ge(S["tM1T"], 16 * idx + mi + 1)
                        ins = eng.activation(out=zb[:, mi, :],
                                             in_=ps_mm[:, (mi % 4) * 512:(mi % 4 + 1) * 512],
                                             func=AF.Silu, bias=b1col(i, mi))
                        ins.then_inc(S["sZEV"], 1)

                if SCHED in (0, 3):
                    for i in range(NL):
                        s_ph(i)
                        for k in range(NCH1):
                            s_g1(i, k)
                        for k in range(NCH1):
                            s_g2(i, k)
                            s_zb(i, k)
                else:
                    s_ph(0)
                    for k in range(NCH1):
                        s_g1(0, k)
                    for i in range(NL):
                        for k in range(NCH1):
                            s_g2(i, k)
                        s_zb(i, 0)
                        if i + 1 < NL:
                            s_ph(i + 1)
                        s_zb(i, 1)
                        s_zb(i, 2)
                        if i + 1 < NL:
                            s_g1(i + 1, 0, part="stats")
                            s_g1(i + 1, 0, part="bc")
                            s_g1(i + 1, 1, part="stats")
                            s_g1(i + 1, 1, part="bc")
                        s_zb(i, 3)
                        if i + 1 < NL:
                            s_g1(i + 1, 2, part="stats")
                            s_g1(i + 1, 2, part="bc")
                            s_g1(i + 1, 3, part="stats")
                            s_g1(i + 1, 3, part="bc")
                eng.sem_inc(S["pB"], 1)

            def p1_tensor(eng):
                eng.wait_ge(S["pB"], 5)

                def t_g1(i, k, part="full"):
                    G1 = 8 * i + k

                    def ep(r_, e_):
                        pass
                    emit_ln("tensor", eng, G1, 0, h[:, :, 0:1], ep, part=part)

                def t_g2(i, k):
                    G2 = 8 * i + 4 + k

                    def ep(r_, e_):
                        pass
                    emit_ln("tensor", eng, G2, 2048, h[:, :, 0:1], ep)

                def t_w1(i, k):
                    idx = NCH1 * i + k
                    eng.wait_ge(S["vY2"], idx + 1)
                    eng.wait_ge(S["qWL"], 16 * (2 * i + 2))
                    if idx > 0:
                        eng.wait_ge(S["vHF"], idx)
                    for mi in range(FB):
                        if mi >= 4:
                            eng.wait_ge(S["sZEV"], 16 * idx + mi - 3)
                        for kb in range(DB):
                            mm = eng.matmul(ps_mm[:, (mi % 4) * 512:(mi % 4 + 1) * 512],
                                            w1ap(i, kb, mi), y2b[idx % 3][:, kb, :],
                                            start=(kb == 0), stop=(kb == DB - 1))
                        mm.then_inc(S["tM1T"], 1)

                def t_w2(i, k):
                    idx = NCH1 * i + k
                    eng.wait_ge(S["sZEV"], 16 * (idx + 1))
                    for mb in range(DB):
                        for fb in range(FB):
                            mm = eng.matmul(ps_mm[:, mb * 512:(mb + 1) * 512],
                                            w2ap(fb, mb), zb[:, fb, :],
                                            start=(fb == 0), stop=(fb == FB - 1))
                        mm.then_inc(S["tM2T"], 1)

                if SCHED in (0, 2, 3):
                    for i in range(NL):
                        for k in range(NCH1):
                            t_g1(i, k)
                        for k in range(NCH1):
                            t_g2(i, k)
                            t_w1(i, k)
                            t_w2(i, k)
                else:
                    for k in range(NCH1):
                        t_g1(0, k)
                    for i in range(NL):
                        for k in range(NCH1):
                            t_g2(i, k)
                        t_w1(i, 0)
                        t_w2(i, 0)
                        t_w1(i, 1)
                        t_w2(i, 1)
                        t_w1(i, 2)
                        t_w2(i, 2)
                        if i + 1 < NL:
                            t_g1(i + 1, 0, part="stats")
                            t_g1(i + 1, 0, part="bc")
                            t_g1(i + 1, 1, part="stats")
                            t_g1(i + 1, 1, part="bc")
                        t_w1(i, 3)
                        t_w2(i, 3)
                        if i + 1 < NL:
                            t_g1(i + 1, 2)
                            t_g1(i + 1, 3)
                eng.sem_inc(S["pB"], 1)

            def p1_gpsimd(eng):
                eng.wait_ge(S["pB"], 5)
                for i in range(NL):
                    eng.wait_ge(S["gBAS"], i + 1)
                    for k in range(NCH1):
                        idx = NCH1 * i + k
                        if DEBUG_TAPS and idx == 0:
                            eng.wait_ge(S["sY1"], 1)
                            eng.dma_start(out=dbg["y00"], in_=yb[:]).then_inc(S["qDBG"], 16)
                        eng.wait_ge(S["vPREP"], idx + 1)
                        if DEBUG_TAPS and idx == 0:
                            eng.dma_start(out=dbg["u0"], in_=ub[:]).then_inc(S["qDBG"], 16)
                            eng.dma_start(out=dbg["C0d"], in_=C0[:]).then_inc(S["qDBG"], 16)
                            eng.dma_start(out=dbg["S0d"], in_=S0[:]).then_inc(S["qDBG"], 16)
                            eng.dma_start(out=dbg["RC0d"], in_=RC0[:]).then_inc(S["qDBG"], 16)
                        eng.wait_ge(S["vSCU"], idx + 1)
                        eng.tensor_mul(q6[:], S0[:], Ub[:])
                        eng.wait_ge(S["vSCAN"], idx + 1)
                        if DEBUG_TAPS and idx == 0:
                            eng.dma_start(out=dbg["U0"], in_=Ub[:]).then_inc(S["qDBG"], 16)
                            eng.dma_start(out=dbg["V0"], in_=Vb[:]).then_inc(S["qDBG"], 16)
                        eng.tensor_mul(ub[:], C0[:], Vb[:])
                        eng.tensor_tensor(q6[:], ub[:], q6[:],
                                          ALU.subtract).then_inc(S["gCMB"], 1)
                    eng.wait_ge(S["vE"], i + 1)
                    eng.dma_start(out=cc_in, in_=ccs[:]).then_inc(S["qCCIN"], 16)
                    eng.wait_ge(S["qCCIN"], 16 * (i + 1))
                    eng.collective_compute(
                        "AllGather", ALU.bypass,
                        replica_groups=[[0, 1, 2, 3, 4, 5, 6, 7]],
                        ins=[cc_in], outs=[cc_out]).then_inc(S["cCC"], 1)
                    eng.wait_ge(S["cCC"], i + 1)
                    eng.dma_start(out=gth[:],
                                  in_=cc_out.rearrange("s p c -> p s c")
                                  ).then_inc(S["qGTH"], 16)
                    if DEBUG_TAPS and i == 0:
                        eng.wait_ge(S["vZ"], 1)
                        eng.dma_start(out=dbg["hc"], in_=h[:]).then_inc(S["qDBG"], 16)
                    if DEBUG_TAPS:
                        eng.wait_ge(S["vHF"], NCH1 * (i + 1))
                        eng.dma_start(out=dbg["hl"][i], in_=h[:]).then_inc(S["qOUT"], 16)
                eng.sem_inc(S["pB"], 1)

            blk1.sync(p1_sync)
            blk1.scalar(p1_scalar)
            blk1.vector(p1_vector)
            blk1.tensor(p1_tensor)
            blk1.gpsimd(p1_gpsimd)

        if STAGE < 3:
            return nc

        es2 = contextlib.ExitStack()
        with es2:
            E2 = es2.enter_context
            hL = E2(nc.sbuf_tensor("hL", [128, DB, TCORE], BF16))
            hb = E2(nc.sbuf_tensor("hb2", [128, DB, TC1], BF16))
            hqb = E2(nc.sbuf_tensor("hqb2", [128, DB, TC1], BF16))
            xh = E2(nc.sbuf_tensor("xh2", [128, DB, TC1], BF16))
            xnslab = E2(nc.sbuf_tensor("xnslab", [128, 4, DKB, TC0], BF16))
            p2stage = [E2(nc.sbuf_tensor(f"p2st{j}", [128, 2048], F32)) for j in range(2)]
            wbb = [E2(nc.sbuf_tensor(f"wbb{j}", [128, DKB * 128], BF16)) for j in range(2)]
            p2ost = [E2(nc.sbuf_tensor(f"p2os{j}", [128, DB * 128], F32)) for j in range(2)]
            wob = [E2(nc.sbuf_tensor(f"wob{j}", [128, DB * 128], BF16)) for j in range(2)]
            xres = [E2(nc.sbuf_tensor(f"xres{j}", [128, 2, TC0], F32)) for j in range(2)]
            outb = [E2(nc.sbuf_tensor(f"outb{j}", [128, 512], F32)) for j in range(2)]

            blk2 = E2(nc.Block("p2"))

            def p2_sync(eng):
                eng.wait_ge(S["pB"], 10)
                for sl in range(2):
                    for m in range(DKB):
                        gm = sl * DKB + m
                        if gm >= 1:
                            eng.wait_ge(S["vWC2"], 3 * (gm - 1) + 1)
                        eng.dma_start(out=p2stage[0][:],
                                      in_=wbase_t[m, :, 0:2048]
                                      ).then_inc(S["qP2A"], 16)
                        for ns in range(2):
                            g = gm * 2 + ns
                            if g >= 2:
                                eng.wait_ge(S["vEV2"], g - 1)
                            c2 = sl * 2 + ns
                            eng.dma_start(out=xres[g % 2][:],
                                          in_=xc[2 * c2:2 * c2 + 2, :, m, :]
                                          .transpose([1, 0, 2])
                                          ).then_inc(S[f"qPX{g % 2}"], 16)

            def p2_gpsimd(eng):
                eng.wait_ge(S["pB"], 10)
                for sl in range(2):
                    if sl > 0:
                        eng.wait_ge(S["tP2"], 64 * sl)
                    for q in range(4):
                        eng.dma_start(out=xnslab[:, q, :, :],
                                      in_=xn_scr[4 * sl + q]).then_inc(S["qXNL"], 16)
                    for m in range(DKB):
                        for ns in range(2):
                            g = (sl * DKB + m) * 2 + ns
                            eng.wait_ge(S["vEV2"], g + 1)
                            c2 = sl * 2 + ns
                            eng.dma_start(out=out_t[m, :, c2 * 512:(c2 + 1) * 512],
                                          in_=outb[g % 2][:]).then_inc(S["qOUT"], 16)
                eng.wait_ge(S["qOUT"], 16 * 128)

            def p2_compute(role, eng):
                eng.wait_ge(S["pB"], 10)
                def mk_ep3(tok):
                    def ep3(r_, e_):
                        if r_ == "vector":
                            for b in range(DB):
                                ins = e_.tensor_scalar(hL[:, b, tok], xh[:, b, :],
                                                       lnlws[:, b:b + 1],
                                                       lnlbs[:, b:b + 1],
                                                       ALU.mult, ALU.add)
                            ins.then_inc(S["vHLs"], 1)
                    return ep3

                if role == "vector":
                    seq = [(0, "copy"), (0, "rows"), (1, "copy"), (0, "norm"),
                           (1, "rows"), (2, "copy"), (1, "norm"), (2, "rows"),
                           (3, "copy"), (2, "norm"), (3, "rows"), (3, "norm")]
                    for kk, part in seq:
                        tok = slice(kk * TC1, (kk + 1) * TC1)
                        emit_ln(role, eng, 32 + kk, 0, h[:, :, tok],
                                mk_ep3(tok), part=part)
                else:
                    for kk in range(NCH1):
                        tok = slice(kk * TC1, (kk + 1) * TC1)
                        emit_ln(role, eng, 32 + kk, 0, h[:, :, tok], mk_ep3(tok))

                if role == "vector":
                    for gm in range(2 * DKB + 1):
                        if gm < 2 * DKB:
                            if gm >= 2:
                                eng.wait_ge(S["tP2"], 2 * (gm - 1))
                            for j_ in range(2):
                                eng.wait_ge(S["qP2A" if j_ == 0 else "qP2B"],
                                            16 * (gm + 1))
                                eng.tensor_copy(wbb[gm % 2][:, j_ * 2048:(j_ + 1) * 2048],
                                                p2stage[j_][:]).then_inc(S["vWC2"], 1)
                            eng.wait_ge(S["qP2O"], 16 * (gm + 1))
                            eng.tensor_copy(wob[gm % 2][:],
                                            p2ost[gm % 2][:]).then_inc(S["vWC2"], 1)
                        if gm >= 1:
                            gp = gm - 1
                            m_e = gp % DKB
                            for ns in range(2):
                                g = gp * 2 + ns
                                eng.wait_ge(S["tP2"], g + 1)
                                eng.wait_ge(S[f"qPX{g % 2}"], 16 * (g // 2 + 1))
                                if g >= 2:
                                    eng.wait_ge(S["qOUT"], 16 * g)
                                pbank = ps_mm[:, (g % 4) * 512:(g % 4 + 1) * 512]
                                ins = eng.scalar_tensor_tensor(
                                    outb[g % 2][:], pbank, bbs[:, m_e:m_e + 1],
                                    xres[g % 2][:].rearrange("p a b -> p (a b)"),
                                    ALU.add, ALU.add)
                                ins.then_inc(S["vEV2"], 1)
                if role == "tensor":
                    for sl in range(2):
                        for m in range(DKB):
                            gm = sl * DKB + m
                            eng.wait_ge(S["vWC2"], 3 * (gm + 1))
                            eng.wait_ge(S["qXNL"], 16 * 4 * (sl + 1))
                            for ns in range(2):
                                g = gm * 2 + ns
                                eng.wait_ge(S["vHLs"], sl * 2 + ns + 1)
                                if g >= 4:
                                    eng.wait_ge(S["vEV2"], g - 3)
                                pbank = ps_mm[:, (g % 4) * 512:(g % 4 + 1) * 512]
                                for kb in range(DB):
                                    eng.matmul(pbank,
                                               wob[gm % 2][:, kb * 128:(kb + 1) * 128],
                                               hL[:, kb, sl * 1024 + ns * 512:
                                                  sl * 1024 + (ns + 1) * 512],
                                               start=(kb == 0), stop=False)
                                for kb in range(DKB):
                                    eng.matmul(pbank,
                                               wbb[gm % 2][:, kb * 128:(kb + 1) * 128],
                                               xnslab[:, 2 * ns:2 * ns + 2, kb, :],
                                               start=False, stop=False)
                                c2 = sl * 2 + ns
                                mm = eng.matmul(pbank, bpb[:, m, :],
                                                hw[:, c2 * 512:(c2 + 1) * 512],
                                                start=False, stop=True)
                                mm.then_inc(S["tP2"], 1)

            def p2_scalar(eng):
                p2_compute("scalar", eng)
                for sl in range(2):
                    for m in range(DKB):
                        gm = sl * DKB + m
                        if gm >= 1:
                            eng.wait_ge(S["vWC2"], 3 * (gm - 1) + 2)
                        eng.dma_start(out=p2stage[1][:],
                                      in_=wbase_t[m, :, 2048:4096]
                                      ).then_inc(S["qP2B"], 16)
                        if gm >= 1:
                            eng.wait_ge(S["vWC2"], 3 * (gm - 1) + 3)
                        eng.dma_start(out=p2ost[gm % 2][:],
                                      in_=wout_t[:, m]).then_inc(S["qP2O"], 16)

            blk2.sync(p2_sync)
            blk2.scalar(p2_scalar)
            blk2.vector(lambda e: p2_compute("vector", e))
            blk2.tensor(lambda e: p2_compute("tensor", e))
            blk2.gpsimd(p2_gpsimd)

    return nc



def _tile_lhsT(W, K, M):
    Mb, Kb = M // 128, K // 128
    t = W.reshape(Mb, 128, Kb, 128).transpose(3, 2, 0, 1)
    return np.ascontiguousarray(t, dtype=np.float32)


def kernel(L0_attn_x, L0_attn_Win, L0_attn_bin, L0_attn_ln1_w, L0_attn_ln1_b,
           L0_attn_phz_re, L0_attn_phz_im, L0_attn_lc_re, L0_attn_lc_im,
           L0_attn_sc_lin, L0_attn_ln2_w, L0_attn_ln2_b, L0_attn_W1, L0_attn_b1,
           L0_attn_W2, L0_attn_b2, L0_attn_lnl_w, L0_attn_lnl_b, L0_attn_Wout,
           L0_attn_bout, L0_ffn_W_base, L0_ffn_b_base, L0_ffn_W_gate, L0_ffn_A,
           L0_ffn_B):
    x = np.asarray(L0_attn_x, dtype=np.float32).reshape(2 * 8192, D)

    def v128(v, n):
        return np.ascontiguousarray(np.asarray(v, np.float32).reshape(n, 128).T)

    win_tile = _tile_lhsT(np.asarray(L0_attn_Win, np.float32), D, DI).reshape(128, 16384)
    w1_tile = np.ascontiguousarray(np.stack(
        [_tile_lhsT(np.asarray(L0_attn_W1[i], np.float32), DI, FF).reshape(128, 8192)
         for i in range(NL)]))
    w2_tile = np.ascontiguousarray(np.stack(
        [_tile_lhsT(np.asarray(L0_attn_W2[i], np.float32), FF, DI).reshape(128, 8192)
         for i in range(NL)]))
    wout_km = _tile_lhsT(np.asarray(L0_attn_Wout, np.float32), DI, D)
    wout_tile = np.ascontiguousarray(wout_km.transpose(0, 2, 1, 3)).reshape(128, DKB, DB * 128)
    wb_km = _tile_lhsT(np.asarray(L0_ffn_W_base, np.float32), D, D)
    wbase_tile = np.ascontiguousarray(wb_km.transpose(2, 0, 1, 3)).reshape(DKB, 128, DKB * 128)

    Wg = np.asarray(L0_ffn_W_gate, np.float32)
    wg_tile = np.ascontiguousarray(Wg.T.reshape(DKB, 128, 8).transpose(1, 0, 2))
    Ap = np.asarray(L0_ffn_A, np.float32).transpose(1, 0, 2).reshape(128, D)
    a_tile = np.ascontiguousarray(Ap.T.reshape(DKB, 128, 128).transpose(1, 0, 2))
    Bp = np.asarray(L0_ffn_B, np.float32).transpose(2, 0, 1).reshape(128, D)
    bp_tile = np.ascontiguousarray(Bp.reshape(128, DKB, 128))

    pars = [L0_attn_ln1_w, L0_attn_ln1_b, L0_attn_phz_re, L0_attn_phz_im,
            L0_attn_lc_re, L0_attn_lc_im, L0_attn_sc_lin, L0_attn_ln2_w,
            L0_attn_ln2_b, L0_attn_b2]
    pch_arr = np.zeros((128, NL, NPAR, DB), np.float32)
    for j, par in enumerate(pars):
        a = np.asarray(par, np.float32)
        pch_arr[:, :, j, :] = a.reshape(NL, DB, 128).transpose(2, 0, 1)
    pch_arr = np.ascontiguousarray(pch_arr.reshape(128, NL * NPAR * DB))
    b1_arr = np.ascontiguousarray(
        np.asarray(L0_attn_b1, np.float32).reshape(NL, FB, 128)
        .transpose(2, 0, 1).reshape(128, NL * FB))

    common = dict(
        win_t=win_tile, w1_t=w1_tile, w2_t=w2_tile, wout_t=wout_tile,
        wbase_t=wbase_tile, wg_t=wg_tile, a_t=a_tile, bp_t=bp_tile,
        pch=pch_arr, b1p=b1_arr,
        binp=v128(L0_attn_bin, DB), lnlwi=v128(L0_attn_lnl_w, DB),
        lnlbi=v128(L0_attn_lnl_b, DB), bbp=v128(L0_ffn_b_base, DKB),
        bop=v128(L0_attn_bout, DKB), identi=np.eye(128, dtype=np.float32),
        sel8i=np.ascontiguousarray(
            (np.arange(128)[None, :] % 8 == np.arange(8)[:, None])
            .astype(np.float32)),
    )

    in_maps = []
    for c in range(NCORES):
        xcpart = x[c * TCORE:(c + 1) * TCORE]
        xc_t = xcpart.reshape(NCH0, TC0, DKB, 128).transpose(0, 3, 2, 1)
        g = c % 4
        mk = np.zeros((128, 29), np.float32)
        mk[:, 0] = 1.0 if g == 0 else 0.0
        mk[:, 1 + g] = 1.0
        base = 4 * (c // 4)
        for k in range(3):
            s = base + g - 1 - k
            if s >= base:
                mk[:, 5 + s * 3 + k] = 1.0
        m = dict(common)
        m["xc"] = np.ascontiguousarray(xc_t, dtype=np.float32)
        m["mki"] = mk
        in_maps.append(m)

    nc = build_nc()
    res = run_bass_kernel_spmd(nc, in_maps, core_ids=list(range(NCORES)))
    global _LAST_RES
    _LAST_RES = res
    outs = []
    for c in range(NCORES):
        arr = np.asarray(res.results[c]["out"])
        outs.append(arr.transpose(2, 0, 1).reshape(TCORE, D))
    return np.concatenate(outs, axis=0).reshape(2, 8192, D).astype(np.float32)


# version 23
# speedup vs baseline: 1.0056x; 1.0056x over previous
import numpy as np
import ml_dtypes
import contextlib
import concourse.bass as bass
import concourse.mybir as mybir
from concourse.bass_utils import run_bass_kernel_spmd

F32 = mybir.dt.float32
BF16 = mybir.dt.bfloat16
ALU = mybir.AluOpType
AF = mybir.ActivationFunctionType
AXX = mybir.AxisListType.X

NCORES = 8
TCORE = 2048
D = 4096
DKB = 32
DI = 512
DB = 4
FF = 2048
FB = 16
NL = 4
TC0, NCH0 = 256, 8
TC1, NCH1 = 512, 4
EPS = 1e-5
NJ = TC0 // 128

P_LN1W, P_LN1B, P_PRE, P_PIM, P_LCR, P_LCI, P_SCL, P_LN2W, P_LN2B, P_B2 = range(10)
NPAR = 10

DEBUG_TAPS = False
SCHED = 1
STAGE = 3
_LAST_RES = None


def build_nc():
    nc = bass.Bass("TRN2", target_bir_lowering=False, debug=False,
                   detect_race_conditions=False)
    Ti = lambda n, s: nc.dram_tensor(n, s, F32, kind="ExternalInput").ap()

    xc = Ti("xc", [NCH0, 128, DKB, TC0])
    win_t = Ti("win_t", [128, 16384])
    w1_t = Ti("w1_t", [NL, 128, 8192])
    w2_t = Ti("w2_t", [NL, 128, 8192])
    wout_t = Ti("wout_t", [128, DKB, DB * 128])
    wbase_t = Ti("wbase_t", [DKB, 128, DKB * 128])
    wg_t = Ti("wg_t", [128, DKB, 8])
    a_t = Ti("a_t", [128, DKB, 128])
    bp_t = Ti("bp_t", [128, DKB, 128])
    pch = Ti("pch", [128, NL * NPAR * DB])
    b1p = Ti("b1p", [128, NL * FB])
    binp = Ti("binp", [128, DB])
    lnlwi = Ti("lnlwi", [128, DB])
    lnlbi = Ti("lnlbi", [128, DB])
    bbp = Ti("bbp", [128, DKB])
    bop = Ti("bop", [128, DKB])
    identi = Ti("identi", [128, 128])
    sel8i = Ti("sel8i", [8, 128])
    mki = Ti("mki", [128, 29])

    out_t = nc.dram_tensor("out", [DKB, 128, TCORE], F32, kind="ExternalOutput").ap()

    xn_scr = nc.dram_tensor("xn_scr", [NCH0, 128, DKB, TC0], BF16, kind="Internal").ap()
    w_scr = nc.dram_tensor("w_scr", [NL, 2, 128, 8192], BF16, kind="Internal").ap()
    cc_in = nc.dram_tensor("cc_in", [128, 8], F32, kind="Internal").ap()
    cc_out = nc.dram_tensor("cc_out", [8, 128, 8], F32, kind="Internal").ap()

    dbg = {}
    if DEBUG_TAPS:
        dbg["xn0"] = nc.dram_tensor("dbg_xn0", [128, DKB, TC0], BF16, kind="ExternalOutput").ap()
        dbg["h0"] = nc.dram_tensor("dbg_h0", [128, DB, TCORE], F32, kind="ExternalOutput").ap()
        dbg["hwt"] = nc.dram_tensor("dbg_hwt", [128, TCORE], BF16, kind="ExternalOutput").ap()
        dbg["hl"] = nc.dram_tensor("dbg_hl", [NL, 128, DB, TCORE], F32, kind="ExternalOutput").ap()
        dbg["y00"] = nc.dram_tensor("dbg_y00", [128, DB, TC1], BF16, kind="ExternalOutput").ap()
        dbg["hA"] = nc.dram_tensor("dbg_hA", [128, TC0], F32, kind="ExternalOutput").ap()
        dbg["lgs"] = nc.dram_tensor("dbg_lgs", [8, TC0], F32, kind="ExternalOutput").ap()
        dbg["rts"] = nc.dram_tensor("dbg_rts", [128, NJ, 8], F32, kind="ExternalOutput").ap()
        dbg["rt2"] = nc.dram_tensor("dbg_rt2", [128, NJ, 8], F32, kind="ExternalOutput").ap()
        dbg["rwxd"] = nc.dram_tensor("dbg_rwxd", [128, TC0], F32, kind="ExternalOutput").ap()
        dbg["rex"] = nc.dram_tensor("dbg_rex", [128, NJ, 8], F32, kind="ExternalOutput").ap()
        dbg["rcl"] = nc.dram_tensor("dbg_rcl", [128, NJ, 4], F32, kind="ExternalOutput").ap()
        dbg["u0"] = nc.dram_tensor("dbg_u0", [128, DB, TC1], BF16, kind="ExternalOutput").ap()
        dbg["U0"] = nc.dram_tensor("dbg_U0", [128, DB, TC1], BF16, kind="ExternalOutput").ap()
        dbg["V0"] = nc.dram_tensor("dbg_V0", [128, DB, TC1], BF16, kind="ExternalOutput").ap()
        dbg["C0d"] = nc.dram_tensor("dbg_C0d", [128, DB, TC1], BF16, kind="ExternalOutput").ap()
        dbg["S0d"] = nc.dram_tensor("dbg_S0d", [128, DB, TC1], BF16, kind="ExternalOutput").ap()
        dbg["RC0d"] = nc.dram_tensor("dbg_RC0d", [128, DB, TC1], BF16, kind="ExternalOutput").ap()
        dbg["hc"] = nc.dram_tensor("dbg_hc", [128, DB, TCORE], F32, kind="ExternalOutput").ap()

    es = contextlib.ExitStack()
    with es:
        E = es.enter_context
        h = E(nc.sbuf_tensor("h", [128, DB, TCORE], F32))
        hw = E(nc.sbuf_tensor("hw", [128, TCORE], BF16))
        pchs = E(nc.sbuf_tensor("pchs", [128, NL * NPAR * DB], F32))
        b1s = E(nc.sbuf_tensor("b1s", [128, NL * FB], F32))
        bins = E(nc.sbuf_tensor("bins", [128, DB], F32))
        lnlws = E(nc.sbuf_tensor("lnlws", [128, DB], F32))
        lnlbs = E(nc.sbuf_tensor("lnlbs", [128, DB], F32))
        bbs = E(nc.sbuf_tensor("bbs", [128, DKB], F32))
        bos = E(nc.sbuf_tensor("bos", [128, DKB], F32))
        ident = E(nc.sbuf_tensor("ident", [128, 128], F32))
        sel8s = E(nc.sbuf_tensor("sel8s", [8, 128], F32))
        mks = E(nc.sbuf_tensor("mks", [128, 29], F32))
        onesb = E(nc.sbuf_tensor("onesb", [128, 1], BF16))
        ones1 = E(nc.sbuf_tensor("ones1", [1, 128], F32))
        rows = E(nc.sbuf_tensor("rows", [1, 2048], F32))
        tny = E(nc.sbuf_tensor("tny", [128, 256], F32))
        tab = E(nc.sbuf_tensor("tab", [128, 13, DB, 2], F32))
        rtab = E(nc.sbuf_tensor("rtab", [128, 13, DB], F32))
        ccs = E(nc.sbuf_tensor("ccs", [128, 8], F32))
        gth = E(nc.sbuf_tensor("gth", [128, 8, 8], F32))
        bpb = E(nc.sbuf_tensor("bpb", [128, DKB, 128], BF16))

        ps_rows = E(nc.psum_tensor("ps_rows", [1, 1024], F32))
        ps_bc = E(nc.psum_tensor("ps_bc", [128, 1024], F32))
        ps_mm = E(nc.psum_tensor("ps_mm", [128, 2048], F32))

        sem_names = [
            "qCON", "qXC", "qSG0", "qSG1", "qWL", "qP2A", "qP2B", "qP2O", "qPX0", "qPX1", "qSPILL", "qRWX", "qGTH", "qCCIN", "qXNL", "qOUT", "qWSP",
            "cCC", "sCW",
            "vCA", "sSQ", "tSQG", "tSUM", "tST0", "vRW0", "sIV0", "vMI0", "tBC0", "vXN", "tPJ",
            "vEV0", "tTR1", "vRTa", "sRTb", "vRTc", "tTR2", "tRWX", "vRWF", "vHW", "vTIN",
            "vLNC", "sLNQ", "tLNS", "vROW", "sINV", "vMIV", "tLNB", "vNRM", "vSCU",
            "sY1", "vY2", "vHLs",
            "gBAS", "vPREP", "vSCAN", "gSCN", "gCMB", "vH1", "vE", "vZ",
            "vPH", "sPH", "tM1T", "sZEV", "tM2T", "vHF",
            "tP2", "vEV2", "vWC2", "qDBG",
            "pB",
        ]
        S = {n: E(nc.semaphore(n)) for n in sem_names}

        loadlist = [("consts", 10)]
        for j_ in range(5):
            loadlist.append((f"sc{j_}", 1))
        for pc in range(8):
            loadlist.append((f"win{pc}", 1))
        wpieces = [(i, m, pc) for i in range(NL) for m in range(2) for pc in range(4)]
        pi = 0
        for c in range(NCH0):
            loadlist.append((f"x{c}", 1))
            for _ in range(4):
                if pi < len(wpieces):
                    i_, m_, p_ = wpieces[pi]
                    loadlist.append((f"wp{i_}_{m_}_{p_}", 1))
                    pi += 1
        for i in range(NL):
            loadlist.append((f"wl{i}_1", 1))
            loadlist.append((f"wl{i}_2", 1))
        for sl in range(2):
            for m in range(DKB):
                loadlist.append((f"p2w_{sl}_{m}", 2))
                loadlist.append((f"p2o_{sl}_{m}", 1))
                loadlist.append((f"p2x_{sl}_{m}_0", 1))
                loadlist.append((f"p2x_{sl}_{m}_1", 1))
        QX = {}
        _acc = 0
        for k_, n_ in loadlist:
            _acc += n_
            QX[k_] = 16 * _acc

        def pcol(i, j, b):
            o = ((i * NPAR) + j) * DB + b
            return pchs[:, o:o + 1]

        def b1col(i, mi):
            return b1s[:, i * FB + mi:i * FB + mi + 1]

        def w1slot(i):
            return [0, 2][i % 2]

        TNY = {}
        _tn = [0]

        def tcol(name, n=1):
            if name not in TNY:
                TNY[name] = _tn[0]
                _tn[0] += n
                assert _tn[0] <= 256
            return tny[:, TNY[name]:TNY[name] + n]

        tcol("Ea", 8)
        TNY["zz"] = TNY["Ea"]
        tcol("Z", 8)
        TNY["rr"] = TNY["Z"]

        def tabv(j, comp):
            return tab[:, j, :, comp]

        def tabc(j, b, comp):
            return tab[:, j, b, comp:comp + 1]

        def rtabv(j):
            return rtab[:, j, :]

        def rtabc(j, b):
            return rtab[:, j, b:b + 1]

        def emit_ln(role, eng, G, toff, h_slice, epilogue, part="full"):
            nt = TC1
            r_mu = rows[0:1, 0:nt]
            r_t = rows[0:1, 512:512 + nt]
            r_var = rows[0:1, 1024:1024 + nt]
            r_inv = rows[0:1, 1536:1536 + nt]
            if role == "vector":
                if part in ("full", "copy"):
                    eng.tensor_copy(hb[:], h_slice).then_inc(S["vLNC"], 1)
                if part in ("full", "rows", "rows1"):
                    eng.wait_ge(S["tLNS"], G + 1)
                    eng.tensor_scalar_mul(r_mu, ps_rows[0:1, 0:nt], -1.0 / DI)
                    eng.tensor_mul(r_t, r_mu, r_mu)
                    eng.scalar_tensor_tensor(r_var, ps_rows[0:1, 512:512 + nt],
                                             1.0 / DI, r_t, ALU.mult, ALU.subtract)
                    eng.tensor_scalar_add(r_t, r_var, EPS)
                    eng.reciprocal(r_var, r_t).then_inc(S["vROW"], 1)
                if part in ("full", "rows", "rows2"):
                    eng.wait_ge(S["sINV"], G + 1)
                    eng.tensor_mul(r_t, r_mu, r_inv).then_inc(S["vMIV"], 1)
                if part in ("full", "norm"):
                    eng.wait_ge(S["tLNB"], G + 1)
                    bc_i = ps_bc[:, 0:nt].unsqueeze(1).broadcast_to([128, DB, nt])
                    bc_m = ps_bc[:, 512:512 + nt].unsqueeze(1).broadcast_to([128, DB, nt])
                    eng.tensor_mul(xh[:], h_slice, bc_i)
                    eng.tensor_tensor(xh[:], bc_m, xh[:], ALU.add).then_inc(S["vNRM"], 1)
            elif role == "scalar":
                if part in ("full", "stats"):
                    eng.wait_ge(S["vLNC"], G + 1)
                    eng.activation(out=hqb[:], in_=hb[:],
                                   func=AF.Square).then_inc(S["sLNQ"], 1)
                if part in ("full", "bc"):
                    eng.wait_ge(S["vROW"], G + 1)
                    eng.activation(out=r_inv, in_=r_var,
                                   func=AF.Sqrt).then_inc(S["sINV"], 1)
            elif role == "tensor":
                if part in ("full", "stats"):
                    eng.wait_ge(S["vLNC"], G + 1)
                    eng.wait_ge(S["sLNQ"], G + 1)
                    if G > 0:
                        eng.wait_ge(S["vROW"], G)
                    for b in range(DB):
                        eng.matmul(ps_rows[0:1, 0:nt], onesb[:], hb[:, b, :],
                                   start=(b == 0), stop=(b == DB - 1))
                    for b in range(DB):
                        mm = eng.matmul(ps_rows[0:1, 512:512 + nt], onesb[:],
                                        hqb[:, b, :],
                                        start=(b == 0), stop=(b == DB - 1))
                    mm.then_inc(S["tLNS"], 1)
                if part in ("full", "bc"):
                    eng.wait_ge(S["vMIV"], G + 1)
                    if G > 0:
                        eng.wait_ge(S["vNRM"], G)
                    eng.matmul(ps_bc[:, 0:nt], ones1[:], r_inv, start=True, stop=True)
                    eng.matmul(ps_bc[:, 512:512 + nt], ones1[:], r_t,
                               start=True, stop=True).then_inc(S["tLNB"], 1)
            if part in ("full", "bc", "norm"):
                epilogue(role, eng)

        def emit_cplx_mul(eng, dst, xa, xb_, t0, t1, t2, t3):
            eng.tensor_mul(t0, xa[:, 0:4], xb_[:, 0:4])
            eng.tensor_mul(t1, xa[:, 4:8], xb_[:, 4:8])
            eng.tensor_mul(t2, xa[:, 0:4], xb_[:, 4:8])
            eng.tensor_mul(t3, xa[:, 4:8], xb_[:, 0:4])
            eng.tensor_tensor(dst[:, 0:4], t0, t1, ALU.subtract)
            eng.tensor_add(dst[:, 4:8], t2, t3)

        es0 = contextlib.ExitStack()
        with es0:
            E0 = es0.enter_context
            stage = [E0(nc.sbuf_tensor(f"stage{j}", [128, 2048], F32)) for j in range(2)]
            castb = [E0(nc.sbuf_tensor(f"castb{j}", [128, 2048], BF16)) for j in range(4)]
            slot0 = [E0(nc.sbuf_tensor(f"wslot{j}", [128, 8192], BF16)) for j in range(2)]
            xch = E0(nc.sbuf_tensor("xch", [128, DKB, TC0], F32))
            xb = E0(nc.sbuf_tensor("xb", [128, DKB, TC0], BF16))
            xnb = E0(nc.sbuf_tensor("xnb", [128, DKB, TC0], BF16))
            xsqp = [E0(nc.sbuf_tensor(f"xsqp{j}", [128, 8, TC0], BF16)) for j in range(2)]
            ablk = E0(nc.sbuf_tensor("ablk", [128, DKB, 128], BF16))
            gblk = E0(nc.sbuf_tensor("gblk", [128, DKB, 8], BF16))
            hAc = E0(nc.sbuf_tensor("hAc", [128, TC0], F32))
            lgs = E0(nc.sbuf_tensor("lgs", [8, TC0], F32))
            rwf = E0(nc.sbuf_tensor("rwf", [8, TC0], F32))
            rts = E0(nc.sbuf_tensor("rts", [128, NJ, 8], F32))
            eqc = E0(nc.sbuf_tensor("eqc", [128, NJ, 8], F32))
            t8 = E0(nc.sbuf_tensor("t8", [128, NJ, 8], F32))
            t9 = E0(nc.sbuf_tensor("t9", [128, NJ, 8], F32))
            expc = E0(nc.sbuf_tensor("expc", [128, NJ, 8], F32))
            on8 = E0(nc.sbuf_tensor("on8", [128, 8], F32))
            zr8 = E0(nc.sbuf_tensor("zr8", [128, 8], F32))
            rscr = E0(nc.sbuf_tensor("rscr", [128, 8], F32))
            mdc = E0(nc.sbuf_tensor("mdc", [128, NJ], F32))
            m1c = E0(nc.sbuf_tensor("m1c", [128, NJ], F32))
            nm1c = E0(nc.sbuf_tensor("nm1c", [128, NJ], F32))
            m2c = E0(nc.sbuf_tensor("m2c", [128, NJ], F32))
            rcpc = E0(nc.sbuf_tensor("rcpc", [128, NJ], F32))

            def win_ap(mb, kb):
                o = (kb * DB + mb) * 128
                return slot0[o // 8192][:, o % 8192:o % 8192 + 128]

            blk0 = E0(nc.Block("p0"))

            @blk0.sync
            def _(eng):
                for key, _n in loadlist:
                    if key.startswith(("wl", "p2")):
                        continue
                    if key == "consts":
                        for dst, src in [(pchs, pch), (b1s, b1p), (bins, binp),
                                         (lnlws, lnlwi), (lnlbs, lnlbi), (bbs, bbp),
                                         (bos, bop), (ident, identi), (mks, mki),
                                         (sel8s, sel8i)]:
                            eng.dma_start(out=dst[:], in_=src).then_inc(S["qCON"], 16)
                    elif key.startswith("sc"):
                        gp = int(key[2:])
                        if gp >= 2:
                            eng.wait_ge(S["sCW"], gp - 1)
                        srcs = [a_t.rearrange("p a b -> p (a b)")[:, 0:2048],
                                a_t.rearrange("p a b -> p (a b)")[:, 2048:4096],
                                bp_t.rearrange("p a b -> p (a b)")[:, 0:2048],
                                bp_t.rearrange("p a b -> p (a b)")[:, 2048:4096],
                                wg_t.rearrange("p a b -> p (a b)")]
                        dstsz = 2048 if gp < 4 else 256
                        eng.dma_start(out=stage[gp % 2][:, 0:dstsz],
                                      in_=srcs[gp]).then_inc(S[f"qSG{gp % 2}"], 16)
                    elif key.startswith("win"):
                        pc = int(key[3:])
                        gp = 5 + pc
                        eng.wait_ge(S["sCW"], gp - 1)
                        eng.dma_start(out=stage[gp % 2][:],
                                      in_=win_t[:, pc * 2048:(pc + 1) * 2048]
                                      ).then_inc(S[f"qSG{gp % 2}"], 16)
                    elif key.startswith("x"):
                        c = int(key[1:])
                        if c >= 1:
                            eng.wait_ge(S["vCA"], c)
                        eng.dma_start(out=xch[:], in_=xc[c]).then_inc(S["qXC"], 16)
                    elif key.startswith("wp"):
                        i_, mt, pc = (int(v) for v in key[2:].split("_"))
                        gp = 13 + (i_ * 2 + mt) * 4 + pc
                        eng.wait_ge(S["sCW"], gp - 1)
                        wsrc = w1_t if mt == 0 else w2_t
                        eng.dma_start(out=stage[gp % 2][:],
                                      in_=wsrc[i_, :, pc * 2048:(pc + 1) * 2048]
                                      ).then_inc(S[f"qSG{gp % 2}"], 16)
                eng.sem_inc(S["pB"], 1)

            @blk0.scalar
            def _(eng):
                abf = ablk[:].rearrange("p a b -> p (a b)")
                bpbf = bpb[:].rearrange("p a b -> p (a b)")
                for gp in range(5):
                    eng.wait_ge(S[f"qSG{gp % 2}"], 16 * (gp // 2 + 1))
                    dsts = [abf[:, 0:2048], abf[:, 2048:4096],
                            bpbf[:, 0:2048], bpbf[:, 2048:4096],
                            gblk[:].rearrange("p a b -> p (a b)")]
                    srcsz = 2048 if gp < 4 else 256
                    eng.copy(dsts[gp], stage[gp % 2][:, 0:srcsz]).then_inc(S["sCW"], 1)
                for pc in range(8):
                    gp = 5 + pc
                    eng.wait_ge(S[f"qSG{gp % 2}"], 16 * (gp // 2 + 1))
                    eng.copy(slot0[pc // 4][:, (pc % 4) * 2048:(pc % 4 + 1) * 2048],
                             stage[gp % 2][:]).then_inc(S["sCW"], 1)
                wq = 0
                for c in range(NCH0):
                    eng.wait_ge(S["vCA"], c + 1)
                    for gq in range(4):
                        if 4 * c + gq >= 2:
                            eng.wait_ge(S["tSQG"], 4 * c + gq - 1)
                        eng.activation(out=xsqp[gq % 2][:],
                                       in_=xb[:, gq * 8:(gq + 1) * 8, :],
                                       func=AF.Square).then_inc(S["sSQ"], 1)
                    eng.wait_ge(S["vRW0"], c + 1)
                    eng.activation(out=rows[0:1, 1536:1536 + TC0],
                                   in_=rows[0:1, 1024:1024 + TC0],
                                   func=AF.Sqrt).then_inc(S["sIV0"], 1)
                    for _ in range(4):
                        if wq < len(wpieces):
                            i_, mt, pc = wpieces[wq]
                            gp = 13 + wq
                            eng.wait_ge(S[f"qSG{gp % 2}"], 16 * (gp // 2 + 1))
                            if wq >= 4:
                                eng.wait_ge(S["qWSP"], 16 * (wq - 3))
                            eng.copy(castb[wq % 4][:],
                                     stage[gp % 2][:]).then_inc(S["sCW"], 1)
                            wq += 1
                    eng.wait_ge(S["vRTa"], c + 1)
                    for j in range(NJ):
                        eng.activation(out=expc[:, j, :], in_=rts[:, j, :],
                                       func=AF.Exp, bias=nm1c[:, j:j + 1])
                    for j in range(NJ):
                        ins = eng.activation(out=rcpc[:, j:j + 1],
                                             in_=mdc[:, j:j + 1], func=AF.Sigmoid)
                    ins.then_inc(S["sRTb"], 1)
                eng.sem_inc(S["pB"], 1)

            @blk0.vector
            def _(eng):
                eng.wait_ge(S["qCON"], 16 * 10)
                eng.memset(onesb[:], 1.0)
                eng.memset(ones1[:], 1.0)
                eng.memset(on8[:], 1.0)
                eng.memset(zr8[:], 0.0)
                eng.memset(rscr[:], 0.0)
                eng.memset(eqc[:], 0.0)
                eng.memset(t8[:], 0.0)
                eng.memset(t9[:], 0.0)
                eng.memset(m1c[:], 0.0)
                eng.memset(m2c[:], 0.0)
                eng.tensor_add(bbs[:], bos[:], bbs[:]).then_inc(S["vTIN"], 1)

                def rpad():
                    eng.scalar_tensor_tensor(rscr[:], on8[:], 1.0, zr8[:],
                                             ALU.mult, ALU.add)
                    eng.scalar_tensor_tensor(rscr[:], on8[:], 1.0, zr8[:],
                                             ALU.mult, ALU.add)

                eng.wait_ge(S["qXC"], 16)
                eng.tensor_copy(xb[:], xch[:]).then_inc(S["vCA"], 1)
                for c in range(NCH0):
                    eng.wait_ge(S["tSUM"], c + 1)
                    eng.tensor_scalar_mul(rows[0:1, 0:TC0], ps_rows[0:1, 0:TC0],
                                          -1.0 / D)
                    eng.tensor_mul(rows[0:1, 512:512 + TC0], rows[0:1, 0:TC0],
                                   rows[0:1, 0:TC0])
                    eng.wait_ge(S["tSQG"], 4 * (c + 1))
                    eng.scalar_tensor_tensor(rows[0:1, 1024:1024 + TC0],
                                             ps_rows[0:1, 512:512 + TC0],
                                             1.0 / D, rows[0:1, 512:512 + TC0],
                                             ALU.mult, ALU.subtract)
                    eng.tensor_scalar_add(rows[0:1, 512:512 + TC0],
                                          rows[0:1, 1024:1024 + TC0], EPS)
                    eng.reciprocal(rows[0:1, 1024:1024 + TC0],
                                   rows[0:1, 512:512 + TC0]).then_inc(S["vRW0"], 1)
                    eng.wait_ge(S["sIV0"], c + 1)
                    eng.tensor_mul(rows[0:1, 1792:1792 + TC0], rows[0:1, 0:TC0],
                                   rows[0:1, 1536:1536 + TC0]).then_inc(S["vMI0"], 1)
                    if c >= 1:
                        eng.wait_ge(S["sRTb"], c)
                        for j in range(NJ):
                            eng.tensor_scalar_sub(eqc[:, j, :], rts[:, j, :],
                                                  m2c[:, j:j + 1])
                        rpad()
                        for j in range(NJ):
                            eng.scalar_tensor_tensor(t8[:, j, :], eqc[:, j, :],
                                                     1e30, on8[:], ALU.mult,
                                                     ALU.add)
                        rpad()
                        for j in range(NJ):
                            eng.tensor_max(t9[:, j, :], t8[:, j, :], zr8[:])
                        rpad()
                        for j in range(NJ):
                            eng.tensor_tensor(eqc[:, j, :], t9[:, j, :], on8[:],
                                              ALU.min)
                        rpad()
                        for j in range(NJ):
                            ins = eng.scalar_tensor_tensor(rts[:, j, :],
                                                           expc[:, j, :],
                                                           rcpc[:, j:j + 1],
                                                           eqc[:, j, :],
                                                           ALU.mult, ALU.mult)
                        ins.then_inc(S["vRTc"], 1)
                        eng.wait_ge(S["tTR2"], c)
                        eng.tensor_copy(rwf[0:8, :],
                                        ps_mm[0:8, 512:512 + TC0]).then_inc(S["vRWF"], 1)
                        eng.wait_ge(S["tRWX"], c)
                        eng.tensor_mul(hw[:, (c - 1) * TC0:c * TC0], hAc[:],
                                       ps_mm[:, 256:512]).then_inc(S["vHW"], 1)
                    eng.wait_ge(S["tBC0"], c + 1)
                    if c >= 1:
                        eng.wait_ge(S["qSPILL"], 16 * c)
                    bc_i = ps_bc[:, 0:TC0].unsqueeze(1).broadcast_to([128, DKB, TC0])
                    bc_m = ps_bc[:, 256:256 + TC0].unsqueeze(1).broadcast_to([128, DKB, TC0])
                    eng.tensor_mul(xnb[:], bc_i, xb[:])
                    eng.tensor_tensor(xnb[:], bc_m, xnb[:],
                                      ALU.add).then_inc(S["vXN"], 1)
                    if c + 1 < NCH0:
                        eng.wait_ge(S["qXC"], 16 * (c + 2))
                        eng.wait_ge(S["tSQG"], 4 * (c + 1))
                        eng.wait_ge(S["sSQ"], 4 * (c + 1))
                        eng.tensor_copy(xb[:], xch[:]).then_inc(S["vCA"], 1)
                    eng.wait_ge(S["tPJ"], c + 1)
                    for mb in range(DB):
                        eng.tensor_scalar(h[:, mb, c * TC0:(c + 1) * TC0],
                                          ps_mm[:, mb * 512:mb * 512 + TC0],
                                          bins[:, mb:mb + 1], None, ALU.add)
                    eng.tensor_copy(hAc[:], ps_bc[:, 512:512 + TC0])
                    eng.tensor_copy(lgs[0:8, :],
                                    ps_bc[0:8, 768:768 + TC0]).then_inc(S["vEV0"], 1)
                    eng.wait_ge(S["tTR1"], c + 1)
                    for j in range(NJ):
                        eng.tensor_copy(rts[:, j, :], ps_mm[:, j * 8:(j + 1) * 8])
                    rpad()
                    for j in range(NJ):
                        eng.reduce_max(m1c[:, j:j + 1], rts[:, j, :], axis=AXX)
                    rpad()
                    for j in range(NJ):
                        eng.tensor_scalar_mul(nm1c[:, j:j + 1], m1c[:, j:j + 1],
                                              -1.0)
                    rpad()
                    for j in range(NJ):
                        eng.tensor_scalar_sub(eqc[:, j, :], rts[:, j, :],
                                              m1c[:, j:j + 1])
                    rpad()
                    for j in range(NJ):
                        eng.scalar_tensor_tensor(t8[:, j, :], eqc[:, j, :], 1e30,
                                                 on8[:], ALU.mult, ALU.add)
                    rpad()
                    for j in range(NJ):
                        eng.tensor_max(t9[:, j, :], t8[:, j, :], zr8[:])
                    rpad()
                    for j in range(NJ):
                        eng.tensor_tensor(eqc[:, j, :], t9[:, j, :], on8[:],
                                          ALU.min)
                    rpad()
                    for j in range(NJ):
                        eng.scalar_tensor_tensor(t8[:, j, :], eqc[:, j, :],
                                                 -1e30, rts[:, j, :],
                                                 ALU.mult, ALU.add)
                    rpad()
                    for j in range(NJ):
                        eng.reduce_max(m2c[:, j:j + 1], t8[:, j, :], axis=AXX)
                    rpad()
                    for j in range(NJ):
                        ins = eng.tensor_tensor(mdc[:, j:j + 1], m1c[:, j:j + 1],
                                                m2c[:, j:j + 1], ALU.subtract)
                    ins.then_inc(S["vRTa"], 1)
                c = NCH0
                eng.wait_ge(S["sRTb"], c)
                for j in range(NJ):
                    eng.tensor_scalar_sub(eqc[:, j, :], rts[:, j, :],
                                          m2c[:, j:j + 1])
                rpad()
                for j in range(NJ):
                    eng.scalar_tensor_tensor(t8[:, j, :], eqc[:, j, :],
                                             1e30, on8[:], ALU.mult,
                                             ALU.add)
                rpad()
                for j in range(NJ):
                    eng.tensor_max(t9[:, j, :], t8[:, j, :], zr8[:])
                rpad()
                for j in range(NJ):
                    eng.tensor_tensor(eqc[:, j, :], t9[:, j, :], on8[:],
                                      ALU.min)
                rpad()
                for j in range(NJ):
                    ins = eng.scalar_tensor_tensor(rts[:, j, :],
                                                   expc[:, j, :],
                                                   rcpc[:, j:j + 1],
                                                   eqc[:, j, :],
                                                   ALU.mult, ALU.mult)
                ins.then_inc(S["vRTc"], 1)
                eng.wait_ge(S["tTR2"], c)
                eng.tensor_copy(rwf[0:8, :],
                                ps_mm[0:8, 512:512 + TC0]).then_inc(S["vRWF"], 1)
                eng.wait_ge(S["tRWX"], c)
                eng.tensor_mul(hw[:, (c - 1) * TC0:c * TC0], hAc[:],
                               ps_mm[:, 256:512]).then_inc(S["vHW"], 1)
                eng.sem_inc(S["pB"], 1)

            @blk0.tensor
            def _(eng):
                eng.wait_ge(S["vCA"], 1)
                for kb in range(DKB):
                    mm = eng.matmul(ps_rows[0:1, 0:TC0], onesb[:], xb[:, kb, :],
                                    start=(kb == 0), stop=(kb == DKB - 1))
                mm.then_inc(S["tSUM"], 1)
                for gq in range(4):
                    eng.wait_ge(S["sSQ"], gq + 1)
                    for kk in range(8):
                        mm = eng.matmul(ps_rows[0:1, 512:512 + TC0], onesb[:],
                                        xsqp[gq % 2][:, kk, :],
                                        start=(gq == 0 and kk == 0),
                                        stop=(gq == 3 and kk == 7))
                    mm.then_inc(S["tSQG"], 1)
                for c in range(NCH0):
                    if c > 0:
                        eng.wait_ge(S["vRTc"], c)
                        eng.wait_ge(S["vEV0"], c)
                        for j in range(NJ):
                            mm = eng.transpose(ps_mm[0:8, 512 + j * 128:512 + (j + 1) * 128],
                                               rts[:, j, :], ident[:])
                        mm.then_inc(S["tTR2"], 1)
                        eng.wait_ge(S["vRWF"], c)
                        if c > 1:
                            eng.wait_ge(S["vHW"], c - 1)
                        eng.matmul(ps_mm[:, 256:512], sel8s[0:8, :], rwf[0:8, 0:TC0],
                                   start=True, stop=True).then_inc(S["tRWX"], 1)
                    eng.wait_ge(S["vMI0"], c + 1)
                    if c > 0:
                        eng.wait_ge(S["vXN"], c)
                    eng.matmul(ps_bc[:, 0:512], ones1[:], rows[0:1, 1536:2048],
                               start=True, stop=True).then_inc(S["tBC0"], 1)
                    eng.wait_ge(S["vXN"], c + 1)
                    eng.wait_ge(S["sCW"], 13)
                    if c > 0:
                        eng.wait_ge(S["vEV0"], c)
                        eng.wait_ge(S["vRTa"], c)
                    for mb in range(DB):
                        for kb in range(DKB):
                            eng.matmul(ps_mm[:, mb * 512:mb * 512 + TC0],
                                       win_ap(mb, kb), xnb[:, kb, :],
                                       start=(kb == 0), stop=(kb == DKB - 1))
                    for kb in range(DKB):
                        eng.matmul(ps_bc[:, 512:512 + TC0], ablk[:, kb, :],
                                   xnb[:, kb, :], start=(kb == 0),
                                   stop=(kb == DKB - 1))
                    for kb in range(DKB):
                        mm = eng.matmul(ps_bc[0:8, 768:768 + TC0], gblk[:, kb, :],
                                        xnb[:, kb, :],
                                        start=(kb == 0), stop=(kb == DKB - 1))
                    mm.then_inc(S["tPJ"], 1)
                    eng.wait_ge(S["vEV0"], c + 1)
                    for j in range(NJ):
                        mm = eng.transpose(ps_mm[:, j * 8:(j + 1) * 8],
                                           lgs[0:8, j * 128:(j + 1) * 128],
                                           ident[0:8, 0:8])
                    mm.then_inc(S["tTR1"], 1)
                    if c + 1 < NCH0:
                        eng.wait_ge(S["vCA"], c + 2)
                        eng.wait_ge(S["vRW0"], c + 1)
                        for kb in range(DKB):
                            mm = eng.matmul(ps_rows[0:1, 0:TC0], onesb[:],
                                            xb[:, kb, :],
                                            start=(kb == 0), stop=(kb == DKB - 1))
                        mm.then_inc(S["tSUM"], 1)
                        for gq in range(4):
                            eng.wait_ge(S["sSQ"], (c + 1) * 4 + gq + 1)
                            for kk in range(8):
                                mm = eng.matmul(ps_rows[0:1, 512:512 + TC0],
                                                onesb[:], xsqp[gq % 2][:, kk, :],
                                                start=(gq == 0 and kk == 0),
                                                stop=(gq == 3 and kk == 7))
                            mm.then_inc(S["tSQG"], 1)
                eng.wait_ge(S["vRTc"], NCH0)
                for j in range(NJ):
                    mm = eng.transpose(ps_mm[0:8, 512 + j * 128:512 + (j + 1) * 128],
                                       rts[:, j, :], ident[:])
                mm.then_inc(S["tTR2"], 1)
                eng.wait_ge(S["vRWF"], NCH0)
                eng.wait_ge(S["vHW"], NCH0 - 1)
                eng.matmul(ps_mm[:, 256:512], sel8s[0:8, :], rwf[0:8, 0:TC0],
                           start=True, stop=True).then_inc(S["tRWX"], 1)
                eng.sem_inc(S["pB"], 1)

            @blk0.gpsimd
            def _(eng):
                def spill(wq):
                    i_, mt, pc = wpieces[wq]
                    eng.wait_ge(S["sCW"], 13 + wq + 1)
                    eng.dma_start(out=w_scr[i_, mt, :, pc * 2048:(pc + 1) * 2048],
                                  in_=castb[wq % 4][:]).then_inc(S["qWSP"], 16)
                for c in range(NCH0):
                    if c >= 1:
                        for wq in range(4 * (c - 1), 4 * c):
                            spill(wq)
                    eng.wait_ge(S["vXN"], c + 1)
                    eng.dma_start(out=xn_scr[c], in_=xnb[:]).then_inc(S["qSPILL"], 16)
                for wq in range(4 * (NCH0 - 1), 32):
                    spill(wq)
                eng.wait_ge(S["qSPILL"], 16 * NCH0)
                eng.wait_ge(S["qWSP"], 16 * 32)
                eng.sem_inc(S["pB"], 1)

        if STAGE < 2:
            return nc

        es1 = contextlib.ExitStack()
        with es1:
            E1 = es1.enter_context
            slot = [E1(nc.sbuf_tensor(f"w1slot{j}", [128, 8192], BF16)) for j in range(3)]
            C0 = E1(nc.sbuf_tensor("C0", [128, DB, TC1], BF16))
            S0 = E1(nc.sbuf_tensor("S0", [128, DB, TC1], BF16))
            RC0 = E1(nc.sbuf_tensor("RC0", [128, DB, TC1], BF16))
            RS0 = E1(nc.sbuf_tensor("RS0", [128, DB, TC1], BF16))
            yb = E1(nc.sbuf_tensor("yb", [128, DB, TC1], BF16))
            ub = E1(nc.sbuf_tensor("ub", [128, DB, TC1], BF16))
            vb = yb
            Ub = E1(nc.sbuf_tensor("Ub", [128, DB, TC1], BF16))
            Vb = E1(nc.sbuf_tensor("Vb", [128, DB, TC1], BF16))
            cyb = E1(nc.sbuf_tensor("cyb", [128, DB, TC1], BF16))
            syb = E1(nc.sbuf_tensor("syb", [128, DB, TC1], BF16))
            qa = E1(nc.sbuf_tensor("qa", [128, DB, TC1], F32))
            hb = E1(nc.sbuf_tensor("hb", [128, DB, TC1], BF16))
            hqb = E1(nc.sbuf_tensor("hqb", [128, DB, TC1], BF16))
            xh = E1(nc.sbuf_tensor("xh", [128, DB, TC1], BF16))
            stgCf = E1(nc.sbuf_tensor("stgCf", [128, DB, TC1], F32))
            stgC = stgCf
            stgS = qa
            q6 = E1(nc.sbuf_tensor("q6", [128, DB, TC1], BF16))
            zb = E1(nc.sbuf_tensor("zb", [128, FB, TC1], BF16))
            y2b = [E1(nc.sbuf_tensor(f"y2b{j}", [128, DB, TC1], BF16))
                   for j in range(3)]

            def w1ap(i, kb, mi):
                o = (kb * FB + mi) * 128
                return slot[w1slot(i)][:, o:o + 128]

            def w2ap(fb, mb):
                o = (fb * DB + mb) * 128
                return slot[1][:, o:o + 128]

            blk1 = E1(nc.Block("p1"))

            def p1_sync(eng):
                eng.wait_ge(S["pB"], 5)
                eng.wait_ge(S["qWSP"], 16 * 32)
                for i in range(NL):
                    if i >= 2:
                        eng.wait_ge(S["tM1T"], 64 * (i - 1) + 16)
                    eng.dma_start(out=slot[w1slot(i)][:],
                                  in_=w_scr[i, 0]).then_inc(S["qWL"], 16)
                    if i >= 1:
                        eng.wait_ge(S["tM2T"], 16 * i)
                    eng.dma_start(out=slot[1][:],
                                  in_=w_scr[i, 1]).then_inc(S["qWL"], 16)
                eng.sem_inc(S["pB"], 1)

            def p1_vector(eng):
                eng.wait_ge(S["pB"], 5)
                colr2 = tcol("r2", DB)
                colrho = tcol("rho", DB)
                colrinv = tcol("rinv", DB)
                colc0 = tcol("c0", DB)
                cols0 = tcol("s0", DB)
                colt0 = tcol("t0", DB)
                colt1 = tcol("Ulast", DB)
                colt3 = tcol("Vlast", DB)
                colt2 = tcol("t2", DB)
                colab = tcol("ab", 8)
                colg1 = tcol("g1", 8)
                colg2 = tcol("g2", 8)
                colg3 = tcol("g3", 8)
                colZ = tcol("Z", 8)
                colw = tcol("w", 8)
                colzw = tcol("zw", 8)
                colEa = tcol("Ea", 8)
                colTk = tcol("Tk", 8)
                colx0 = tcol("x0", DB)
                colx1 = tcol("x1", DB)
                colcm2 = tcol("cm2", DB)
                colcm3 = tcol("cm3", DB)
                colTk1 = tcol("Tk1", 8)
                colTk2 = tcol("Tk2", 8)
                colc1 = tcol("c1", 8)
                colc2v = tcol("c2", 8)
                def v_ph(i):
                    if i > 0:
                        eng.wait_ge(S["sPH"], i)
                    for b in range(DB):
                        eng.tensor_mul(colx0[:, b:b + 1], pcol(i, P_PRE, b),
                                       pcol(i, P_PRE, b))
                    for b in range(DB):
                        eng.tensor_mul(colx1[:, b:b + 1], pcol(i, P_PIM, b),
                                       pcol(i, P_PIM, b))
                    for b in range(DB):
                        ins = eng.tensor_add(colr2[:, b:b + 1], colx0[:, b:b + 1],
                                             colx1[:, b:b + 1])
                    ins.then_inc(S["vPH"], 1)
                    eng.wait_ge(S["sPH"], i + 1)
                    eng.reciprocal(colrinv, tcol("rr", DB))
                    eng.tensor_copy(rtabv(0), colrho)
                    for b in range(DB):
                        eng.tensor_mul(colc0[:, b:b + 1], pcol(i, P_PRE, b),
                                       colrinv[:, b:b + 1])
                    for b in range(DB):
                        eng.tensor_mul(cols0[:, b:b + 1], pcol(i, P_PIM, b),
                                       colrinv[:, b:b + 1])
                    eng.tensor_copy(tabv(0, 0), colc0)
                    eng.tensor_copy(tabv(0, 1), cols0)
                    eng.memset(stgC[:, :, 0:1], 1.0)
                    eng.memset(stgS[:, :, 0:1], 0.0)
                    for j in range(12):
                        eng.tensor_mul(colx0, tabv(j, 0), tabv(j, 0))
                        eng.tensor_mul(colx1, tabv(j, 1), tabv(j, 1))
                        eng.tensor_mul(colt2, tabv(j, 0), tabv(j, 1))
                        eng.tensor_mul(rtabv(j + 1), rtabv(j), rtabv(j))
                        eng.tensor_tensor(tabv(j + 1, 0), colx0, colx1, ALU.subtract)
                        eng.tensor_add(tabv(j + 1, 1), colt2, colt2)
                    eng.memset(colab[:, 0:4], 1.0)
                    eng.memset(colab[:, 4:8], 0.0)
                    eng.tensor_mul(colg1[:, 0:4], rtabv(11), tabv(11, 0))
                    eng.tensor_mul(colg1[:, 4:8], rtabv(11), tabv(11, 1))
                    emit_cplx_mul(eng, colg2, colg1, colg1, colx0, colx1,
                                  colcm2, colcm3)
                    emit_cplx_mul(eng, colg3, colg2, colg1, colx0, colx1,
                                  colcm2, colcm3)
                def v_bases(i):
                    if i > 0:
                        eng.wait_ge(S["gCMB"], NCH1 * i)
                    for basekind in range(2):
                        if basekind == 1:
                            eng.memset(stgC[:, :, 0:1], 1.0)
                            eng.memset(stgS[:, :, 0:1], 0.0)
                        n, j = 1, 0
                        while n < TC1:
                            if basekind == 0:
                                fc = [tabc(j, b, 0) for b in range(DB)]
                                fs = [tabc(j, b, 1) for b in range(DB)]
                            else:
                                for b in range(DB):
                                    eng.tensor_mul(colx0[:, b:b + 1], rtabc(j, b),
                                                   tabc(j, b, 0))
                                for b in range(DB):
                                    eng.tensor_mul(colx1[:, b:b + 1], rtabc(j, b),
                                                   tabc(j, b, 1))
                                fc = [colx0[:, b:b + 1] for b in range(DB)]
                                fs = [colx1[:, b:b + 1] for b in range(DB)]
                            for b in range(DB):
                                eng.tensor_scalar_mul(stgC[:, b, n:2 * n],
                                                      stgS[:, b, 0:n], fs[b])
                            for b in range(DB):
                                eng.scalar_tensor_tensor(stgC[:, b, n:2 * n],
                                                         stgC[:, b, 0:n], fc[b],
                                                         stgC[:, b, n:2 * n],
                                                         ALU.mult, ALU.subtract)
                            for b in range(DB):
                                eng.tensor_scalar_mul(stgS[:, b, n:2 * n],
                                                      stgC[:, b, 0:n], fs[b])
                            for b in range(DB):
                                eng.scalar_tensor_tensor(stgS[:, b, n:2 * n],
                                                         stgS[:, b, 0:n], fc[b],
                                                         stgS[:, b, n:2 * n],
                                                         ALU.mult, ALU.add)
                            n *= 2
                            j += 1
                        if basekind == 0:
                            eng.tensor_copy(C0[:], stgC[:])
                            eng.tensor_copy(S0[:], stgS[:])
                        else:
                            eng.tensor_copy(RC0[:], stgC[:])
                            eng.tensor_scalar_mul(RS0[:], stgS[:], -1.0).then_inc(S["gBAS"], 1)
                def v_sconv(i, k):
                    idx = NCH1 * i + k
                    G1 = 8 * i + k
                    tok = slice(k * TC1, (k + 1) * TC1)

                    def ep1(r_, e_, i=i):
                        pass
                    emit_ln("vector", eng, G1, 0, h[:, :, tok], ep1, part="rows1")
                    if k < NCH1 - 1:
                        tok2 = slice((k + 1) * TC1, (k + 2) * TC1)
                        emit_ln("vector", eng, G1 + 1, 0, h[:, :, tok2], ep1,
                                part="copy")
                    emit_ln("vector", eng, G1, 0, h[:, :, tok], ep1, part="rows2")
                    emit_ln("vector", eng, G1, 0, h[:, :, tok], ep1, part="norm")
                    eng.wait_ge(S["sY1"], 2 * idx + 1)
                    eng.tensor_mul(cyb[:, 0:2, :], C0[:, 0:2, :], yb[:, 0:2, :])
                    eng.tensor_mul(syb[:, 0:2, :], S0[:, 0:2, :], yb[:, 0:2, :])
                    for b in range(2):
                        eng.tensor_scalar_mul(ub[:, b, :], syb[:, b, :],
                                              colab[:, 4 + b:4 + b + 1])
                    eng.wait_ge(S["sY1"], 2 * (idx + 1))
                    eng.tensor_mul(cyb[:, 2:4, :], C0[:, 2:4, :], yb[:, 2:4, :])
                    eng.tensor_mul(syb[:, 2:4, :], S0[:, 2:4, :], yb[:, 2:4, :])
                    for b in range(2, DB):
                        eng.tensor_scalar_mul(ub[:, b, :], syb[:, b, :],
                                              colab[:, 4 + b:4 + b + 1])
                    for b in range(DB):
                        eng.scalar_tensor_tensor(ub[:, b, :], cyb[:, b, :],
                                                 colab[:, b:b + 1], ub[:, b, :],
                                                 ALU.mult, ALU.subtract)
                    for b in range(DB):
                        init = 0.0 if k == 0 else colt1[:, b:b + 1]
                        ins = eng.tensor_tensor_scan(
                            Ub[:, b, :],
                            colrho[:, b:b + 1].broadcast_to([128, TC1]),
                            ub[:, b, :], init, ALU.mult, ALU.add)
                    ins.then_inc(S["vSCU"], 1)
                    for b in range(DB):
                        eng.tensor_scalar_mul(vb[:, b, :], cyb[:, b, :],
                                              colab[:, 4 + b:4 + b + 1])
                    for b in range(DB):
                        ins = eng.scalar_tensor_tensor(vb[:, b, :], syb[:, b, :],
                                                       colab[:, b:b + 1],
                                                       vb[:, b, :], ALU.mult,
                                                       ALU.add)
                    ins.then_inc(S["vPREP"], 1)
                    for b in range(DB):
                        initv = 0.0 if k == 0 else colt3[:, b:b + 1]
                        eng.tensor_tensor_scan(
                            Vb[:, b, :],
                            colrho[:, b:b + 1].broadcast_to([128, TC1]),
                            vb[:, b, :], initv, ALU.mult, ALU.add)
                    for b in range(DB):
                        eng.tensor_copy(colt1[:, b:b + 1], Ub[:, b, TC1 - 1:TC1])
                    for b in range(DB):
                        ins = eng.tensor_copy(colt3[:, b:b + 1],
                                              Vb[:, b, TC1 - 1:TC1])
                    ins.then_inc(S["vSCAN"], 1)
                    eng.tensor_mul(syb[:], C0[:], Ub[:])
                    eng.tensor_mul(cyb[:], S0[:], Vb[:])
                    eng.tensor_tensor(syb[:], cyb[:], syb[:], ALU.add)
                    eng.wait_ge(S["gCMB"], idx + 1)
                    for b in range(DB):
                        eng.tensor_scalar_mul(ub[:, b, :], q6[:, b, :],
                                              colab[:, 4 + b:4 + b + 1])
                    for b in range(DB):
                        eng.scalar_tensor_tensor(ub[:, b, :], syb[:, b, :],
                                                 colab[:, b:b + 1], ub[:, b, :],
                                                 ALU.mult, ALU.add)
                    for b in range(DB):
                        ins = eng.scalar_tensor_tensor(h[:, b, tok], ub[:, b, :],
                                                       pcol(i, P_SCL, b),
                                                       h[:, b, tok],
                                                       ALU.mult, ALU.add)
                    ins.then_inc(S["vH1"], 1)
                    if k < NCH1 - 1:
                        zz = tcol("zz", 8)
                        eng.tensor_mul(colx0, colab[:, 0:4], tabv(9, 1))
                        eng.tensor_mul(colx1, colab[:, 4:8], tabv(9, 1))
                        eng.tensor_mul(zz[:, 0:4], colab[:, 0:4], tabv(9, 0))
                        eng.tensor_mul(zz[:, 4:8], colab[:, 4:8], tabv(9, 0))
                        eng.tensor_tensor(colab[:, 0:4], zz[:, 0:4],
                                          colx1, ALU.subtract)
                        eng.tensor_tensor(colab[:, 4:8], zz[:, 4:8],
                                          colx0, ALU.add)

                def v_chain(i):
                    eC = tcol("eC", DB)
                    eS_ = tcol("eS", DB)
                    eng.tensor_mul(colx0, tabv(11, 0), tabv(0, 0))
                    eng.tensor_mul(colx1, tabv(11, 1), tabv(0, 1))
                    eng.tensor_mul(colcm2, tabv(11, 1), tabv(0, 0))
                    eng.tensor_mul(colcm3, tabv(11, 0), tabv(0, 1))
                    eng.tensor_add(eC, colx0, colx1)
                    eng.tensor_tensor(eS_, colcm2, colcm3, ALU.subtract)
                    eng.tensor_mul(colx0, eC, colt1)
                    eng.tensor_mul(colx1, eS_, colt3)
                    eng.tensor_mul(colcm2, eS_, colt1)
                    eng.tensor_mul(colcm3, eC, colt3)
                    eng.tensor_add(colEa[:, 0:4], colx0, colx1)
                    eng.tensor_tensor(colEa[:, 4:8], colcm2, colcm3, ALU.subtract)
                    for b in range(DB):
                        eng.tensor_copy(ccs[:, 2 * b:2 * b + 1], colEa[:, b:b + 1])
                    for b in range(DB):
                        ins = eng.tensor_copy(ccs[:, 2 * b + 1:2 * b + 2],
                                              colEa[:, 4 + b:4 + b + 1])
                    ins.then_inc(S["vE"], 1)
                    eng.wait_ge(S["cCC"], i + 1)
                    eng.wait_ge(S["qGTH"], 16 * (i + 1))
                    TKB = [colTk, colTk1, colTk2]
                    for s_ in range(8):
                        for kk in range(3):
                            msk = mks[:, 5 + s_ * 3 + kk:5 + s_ * 3 + kk + 1]
                            if s_ == 0:
                                eng.tensor_scalar_mul(TKB[kk][:], gth[:, s_, :], msk)
                            else:
                                eng.scalar_tensor_tensor(TKB[kk][:], gth[:, s_, :],
                                                         msk, TKB[kk][:],
                                                         ALU.mult, ALU.add)
                    for g, Tsrc, cdst in ((colg1, colTk1, colc1),
                                          (colg2, colTk2, colc2v)):
                        for b in range(DB):
                            eng.tensor_mul(colx0[:, b:b + 1], g[:, b:b + 1],
                                           Tsrc[:, 2 * b:2 * b + 1])
                        for b in range(DB):
                            eng.tensor_mul(colx1[:, b:b + 1], g[:, 4 + b:4 + b + 1],
                                           Tsrc[:, 2 * b + 1:2 * b + 2])
                        for b in range(DB):
                            eng.tensor_mul(colcm2[:, b:b + 1], g[:, b:b + 1],
                                           Tsrc[:, 2 * b + 1:2 * b + 2])
                        for b in range(DB):
                            eng.tensor_mul(colcm3[:, b:b + 1], g[:, 4 + b:4 + b + 1],
                                           Tsrc[:, 2 * b:2 * b + 1])
                        for b in range(DB):
                            eng.tensor_tensor(cdst[:, b:b + 1], colx0[:, b:b + 1],
                                              colx1[:, b:b + 1], ALU.subtract)
                        for b in range(DB):
                            eng.tensor_add(cdst[:, 4 + b:4 + b + 1],
                                           colcm2[:, b:b + 1], colcm3[:, b:b + 1])
                    for b in range(DB):
                        eng.tensor_add(colZ[:, b:b + 1], colTk[:, 2 * b:2 * b + 1],
                                       colc1[:, b:b + 1])
                    for b in range(DB):
                        eng.tensor_add(colZ[:, 4 + b:4 + b + 1],
                                       colTk[:, 2 * b + 1:2 * b + 2],
                                       colc1[:, 4 + b:4 + b + 1])
                    for b in range(DB):
                        eng.tensor_add(colZ[:, b:b + 1], colc2v[:, b:b + 1],
                                       colZ[:, b:b + 1])
                    for b in range(DB):
                        eng.tensor_add(colZ[:, 4 + b:4 + b + 1],
                                       colc2v[:, 4 + b:4 + b + 1],
                                       colZ[:, 4 + b:4 + b + 1])
                    glr = tcol("glr", DB)
                    gli = tcol("gli", DB)
                    eng.tensor_scalar(glr, mks[:, 1:2].broadcast_to([128, DB]), 1.0,
                                      None, ALU.mult)
                    eng.tensor_scalar(gli, mks[:, 1:2].broadcast_to([128, DB]), 0.0,
                                      None, ALU.mult)
                    for kk, g in [(1, colg1), (2, colg2), (3, colg3)]:
                        sel = mks[:, 1 + kk:2 + kk]
                        for b in range(DB):
                            eng.scalar_tensor_tensor(glr[:, b:b + 1], g[:, b:b + 1],
                                                     sel, glr[:, b:b + 1],
                                                     ALU.mult, ALU.add)
                        for b in range(DB):
                            eng.scalar_tensor_tensor(gli[:, b:b + 1],
                                                     g[:, 4 + b:4 + b + 1], sel,
                                                     gli[:, b:b + 1], ALU.mult,
                                                     ALU.add)
                    for b in range(DB):
                        eng.tensor_mul(colx0[:, b:b + 1], glr[:, b:b + 1],
                                       pcol(i, P_LCR, b))
                    for b in range(DB):
                        eng.tensor_mul(colx1[:, b:b + 1], gli[:, b:b + 1],
                                       pcol(i, P_LCI, b))
                    for b in range(DB):
                        eng.tensor_mul(colcm2[:, b:b + 1], glr[:, b:b + 1],
                                       pcol(i, P_LCI, b))
                    for b in range(DB):
                        eng.tensor_mul(colcm3[:, b:b + 1], gli[:, b:b + 1],
                                       pcol(i, P_LCR, b))
                    for b in range(DB):
                        eng.tensor_tensor(colx0[:, b:b + 1], colx0[:, b:b + 1],
                                          colx1[:, b:b + 1], ALU.subtract)
                    for b in range(DB):
                        eng.tensor_add(colcm2[:, b:b + 1], colcm2[:, b:b + 1],
                                       colcm3[:, b:b + 1])
                    for b in range(DB):
                        eng.tensor_add(colZ[:, b:b + 1], colx0[:, b:b + 1],
                                       colZ[:, b:b + 1])
                    for b in range(DB):
                        eng.tensor_add(colZ[:, 4 + b:4 + b + 1], colcm2[:, b:b + 1],
                                       colZ[:, 4 + b:4 + b + 1])
                    pr = tcol("p_r", DB)
                    pim = tcol("p_i", DB)
                    eng.tensor_mul(pr, rtabv(0), tabv(0, 0))
                    eng.tensor_mul(pim, rtabv(0), tabv(0, 1))
                    for b in range(DB):
                        eng.tensor_mul(colx0[:, b:b + 1], pr[:, b:b + 1],
                                       colZ[:, b:b + 1])
                    for b in range(DB):
                        eng.tensor_mul(colx1[:, b:b + 1], pim[:, b:b + 1],
                                       colZ[:, 4 + b:4 + b + 1])
                    for b in range(DB):
                        eng.tensor_mul(colcm2[:, b:b + 1], pr[:, b:b + 1],
                                       colZ[:, 4 + b:4 + b + 1])
                    for b in range(DB):
                        eng.tensor_mul(colcm3[:, b:b + 1], pim[:, b:b + 1],
                                       colZ[:, b:b + 1])
                    for b in range(DB):
                        eng.tensor_tensor(colw[:, b:b + 1], colx0[:, b:b + 1],
                                          colx1[:, b:b + 1], ALU.subtract)
                    for b in range(DB):
                        eng.tensor_add(colw[:, 4 + b:4 + b + 1], colcm2[:, b:b + 1],
                                       colcm3[:, b:b + 1])
                    for b in range(DB):
                        eng.tensor_mul(colzw[:, b:b + 1], colw[:, b:b + 1],
                                       pcol(i, P_SCL, b))
                    for b in range(DB):
                        eng.tensor_mul(colzw[:, 4 + b:4 + b + 1],
                                       colw[:, 4 + b:4 + b + 1], pcol(i, P_SCL, b))
                    for k in range(NCH1):
                        tok = slice(k * TC1, (k + 1) * TC1)
                        for b in range(DB):
                            eng.scalar_tensor_tensor(h[:, b, tok], RC0[:, b, :],
                                                     colzw[:, b:b + 1], h[:, b, tok],
                                                     ALU.mult, ALU.add)
                            eng.scalar_tensor_tensor(h[:, b, tok], RS0[:, b, :],
                                                     colzw[:, 4 + b:4 + b + 1],
                                                     h[:, b, tok],
                                                     ALU.mult, ALU.add)
                        if k == 0:
                            emit_ln("vector", eng, 8 * i + 4, 2048,
                                    h[:, :, 0:TC1], lambda r_, e_: None,
                                    part="copy")
                        if k < NCH1 - 1:
                            zz = tcol("zz", 8)
                            eng.tensor_mul(colx0, colzw[:, 0:4], tabv(9, 1))
                            eng.tensor_mul(colx1, colzw[:, 4:8], tabv(9, 1))
                            eng.tensor_mul(zz[:, 0:4], colzw[:, 0:4], tabv(9, 0))
                            eng.tensor_mul(zz[:, 4:8], colzw[:, 4:8], tabv(9, 0))
                            eng.tensor_tensor(colx1, zz[:, 0:4], colx1, ALU.subtract)
                            eng.tensor_tensor(colx0, zz[:, 4:8], colx0, ALU.add)
                            eng.tensor_mul(colzw[:, 0:4], colx1, rtabv(9))
                            eng.tensor_mul(colzw[:, 4:8], colx0, rtabv(9))
                    eng.engine_nop().then_inc(S["vZ"], 1)
                    if DEBUG_TAPS and i == 0:
                        eng.wait_ge(S["qDBG"], 16 * 8)
                def v_ln2(i, k, part="full"):
                    idx = NCH1 * i + k
                    G2 = 8 * i + 4 + k
                    tok = slice(k * TC1, (k + 1) * TC1)

                    def ep2(r_, e_, i=i, idx=idx):
                        if r_ == "vector":
                            if idx >= 3:
                                e_.wait_ge(S["tM1T"], 16 * (idx - 2))
                            for b in range(DB):
                                ins2 = e_.tensor_scalar(y2b[idx % 3][:, b, :],
                                                        xh[:, b, :],
                                                        pcol(i, P_LN2W, b),
                                                        pcol(i, P_LN2B, b),
                                                        ALU.mult, ALU.add)
                            ins2.then_inc(S["vY2"], 1)
                    emit_ln("vector", eng, G2, 2048, h[:, :, tok], ep2, part=part)

                def v_hupd(i, k):
                    idx = NCH1 * i + k
                    tok = slice(k * TC1, (k + 1) * TC1)
                    eng.wait_ge(S["tM2T"], 4 * (idx + 1))
                    for mb in range(DB):
                        ins = eng.scalar_tensor_tensor(h[:, mb, tok],
                                                       ps_mm[:, mb * 512:(mb + 1) * 512],
                                                       pcol(i, P_B2, mb),
                                                       h[:, mb, tok],
                                                       ALU.add, ALU.add)
                    ins.then_inc(S["vHF"], 1)

                def v_lncopy(i, k):
                    G1 = 8 * i + k
                    tok = slice(k * TC1, (k + 1) * TC1)
                    emit_ln("vector", eng, G1, 0, h[:, :, tok],
                            lambda r_, e_: None, part="copy")

                def v_ln2batch(i):
                    v_ln2(i, 0, "rows")
                    v_ln2(i, 1, "copy")
                    v_ln2(i, 0, "norm")
                    v_ln2(i, 1, "rows")
                    v_ln2(i, 2, "copy")
                    v_ln2(i, 1, "norm")
                    v_ln2(i, 2, "rows")
                    v_ln2(i, 3, "copy")
                    v_ln2(i, 2, "norm")
                    v_ln2(i, 3, "rows")
                    v_ln2(i, 3, "norm")

                if SCHED == 0:
                    for i in range(NL):
                        v_ph(i)
                        v_bases(i)
                        for k in range(NCH1):
                            v_sconv(i, k)
                        v_chain(i)
                        for k in range(NCH1):
                            v_ln2(i, k)
                            v_hupd(i, k)
                else:
                    v_ph(0)
                    v_bases(0)
                    v_lncopy(0, 0)
                    for k in range(NCH1):
                        v_sconv(0, k)
                    v_chain(0)
                    for i in range(NL):
                        v_ln2batch(i)
                        if i + 1 < NL:
                            v_ph(i + 1)
                        v_hupd(i, 0)
                        v_hupd(i, 1)
                        if i + 1 < NL:
                            v_bases(i + 1)
                            v_lncopy(i + 1, 0)
                            v_sconv(i + 1, 0)
                        v_hupd(i, 2)
                        if i + 1 < NL:
                            v_sconv(i + 1, 1)
                        v_hupd(i, 3)
                        if i + 1 < NL:
                            v_sconv(i + 1, 2)
                            v_sconv(i + 1, 3)
                            v_chain(i + 1)
                eng.sem_inc(S["pB"], 1)

            def p1_scalar(eng):
                eng.wait_ge(S["pB"], 5)
                colr2 = tcol("r2", DB)
                colrr = tcol("rr", DB)
                colrho = tcol("rho", DB)
                colrinv = tcol("rinv", DB)

                def s_ph(i):
                    eng.wait_ge(S["vPH"], i + 1)
                    eng.activation(out=colrr, in_=colr2, func=AF.Sqrt)
                    eng.activation(out=colrinv, in_=colr2, func=AF.Square)
                    eng.activation(out=colrho, in_=colrr,
                                   func=AF.Tanh).then_inc(S["sPH"], 1)

                def s_g1(i, k, part="full"):
                    G1 = 8 * i + k
                    idx = NCH1 * i + k

                    def ep1(r_, e_, i=i, idx=idx, G1=G1):
                        if r_ == "scalar":
                            e_.wait_ge(S["vNRM"], G1 + 1)
                            if idx > 0:
                                e_.wait_ge(S["vSCAN"], idx)
                            for b in range(DB):
                                ins = e_.activation(out=yb[:, b, :],
                                                    in_=xh[:, b, :], func=AF.Silu,
                                                    bias=pcol(i, P_LN1B, b),
                                                    scale=pcol(i, P_LN1W, b))
                                if b == 1:
                                    ins.then_inc(S["sY1"], 1)
                            ins.then_inc(S["sY1"], 1)
                    emit_ln("scalar", eng, G1, 0, h[:, :, 0:1], ep1, part=part)

                def s_g2(i, k):
                    G2 = 8 * i + 4 + k

                    def ep2(r_, e_):
                        pass
                    emit_ln("scalar", eng, G2, 2048, h[:, :, 0:1], ep2)

                def s_zb(i, k):
                    idx = NCH1 * i + k
                    if idx > 0:
                        eng.wait_ge(S["tM2T"], 4 * idx)
                    for mi in range(FB):
                        eng.wait_ge(S["tM1T"], 16 * idx + mi + 1)
                        ins = eng.activation(out=zb[:, mi, :],
                                             in_=ps_mm[:, (mi % 4) * 512:(mi % 4 + 1) * 512],
                                             func=AF.Silu, bias=b1col(i, mi))
                        ins.then_inc(S["sZEV"], 1)

                if SCHED in (0, 3):
                    for i in range(NL):
                        s_ph(i)
                        for k in range(NCH1):
                            s_g1(i, k)
                        for k in range(NCH1):
                            s_g2(i, k)
                            s_zb(i, k)
                else:
                    s_ph(0)
                    for k in range(NCH1):
                        s_g1(0, k)
                    for i in range(NL):
                        for k in range(NCH1):
                            s_g2(i, k)
                        s_zb(i, 0)
                        if i + 1 < NL:
                            s_ph(i + 1)
                        s_zb(i, 1)
                        s_zb(i, 2)
                        if i + 1 < NL:
                            s_g1(i + 1, 0, part="stats")
                            s_g1(i + 1, 0, part="bc")
                            s_g1(i + 1, 1, part="stats")
                            s_g1(i + 1, 1, part="bc")
                        s_zb(i, 3)
                        if i + 1 < NL:
                            s_g1(i + 1, 2, part="stats")
                            s_g1(i + 1, 2, part="bc")
                            s_g1(i + 1, 3, part="stats")
                            s_g1(i + 1, 3, part="bc")
                eng.sem_inc(S["pB"], 1)

            def p1_tensor(eng):
                eng.wait_ge(S["pB"], 5)

                def t_g1(i, k, part="full"):
                    G1 = 8 * i + k

                    def ep(r_, e_):
                        pass
                    emit_ln("tensor", eng, G1, 0, h[:, :, 0:1], ep, part=part)

                def t_g2(i, k):
                    G2 = 8 * i + 4 + k

                    def ep(r_, e_):
                        pass
                    emit_ln("tensor", eng, G2, 2048, h[:, :, 0:1], ep)

                def t_w1(i, k):
                    idx = NCH1 * i + k
                    eng.wait_ge(S["vY2"], idx + 1)
                    eng.wait_ge(S["qWL"], 16 * (2 * i + 2))
                    if idx > 0:
                        eng.wait_ge(S["vHF"], idx)
                    for mi in range(FB):
                        if mi >= 4:
                            eng.wait_ge(S["sZEV"], 16 * idx + mi - 3)
                        for kb in range(DB):
                            mm = eng.matmul(ps_mm[:, (mi % 4) * 512:(mi % 4 + 1) * 512],
                                            w1ap(i, kb, mi), y2b[idx % 3][:, kb, :],
                                            start=(kb == 0), stop=(kb == DB - 1))
                        mm.then_inc(S["tM1T"], 1)

                def t_w2(i, k):
                    idx = NCH1 * i + k
                    eng.wait_ge(S["sZEV"], 16 * (idx + 1))
                    for mb in range(DB):
                        for fb in range(FB):
                            mm = eng.matmul(ps_mm[:, mb * 512:(mb + 1) * 512],
                                            w2ap(fb, mb), zb[:, fb, :],
                                            start=(fb == 0), stop=(fb == FB - 1))
                        mm.then_inc(S["tM2T"], 1)

                if SCHED in (0, 2, 3):
                    for i in range(NL):
                        for k in range(NCH1):
                            t_g1(i, k)
                        for k in range(NCH1):
                            t_g2(i, k)
                            t_w1(i, k)
                            t_w2(i, k)
                else:
                    for k in range(NCH1):
                        t_g1(0, k)
                    for i in range(NL):
                        for k in range(NCH1):
                            t_g2(i, k)
                        t_w1(i, 0)
                        t_w2(i, 0)
                        t_w1(i, 1)
                        t_w2(i, 1)
                        t_w1(i, 2)
                        t_w2(i, 2)
                        if i + 1 < NL:
                            t_g1(i + 1, 0, part="stats")
                            t_g1(i + 1, 0, part="bc")
                            t_g1(i + 1, 1, part="stats")
                            t_g1(i + 1, 1, part="bc")
                        t_w1(i, 3)
                        t_w2(i, 3)
                        if i + 1 < NL:
                            t_g1(i + 1, 2)
                            t_g1(i + 1, 3)
                eng.sem_inc(S["pB"], 1)

            def p1_gpsimd(eng):
                eng.wait_ge(S["pB"], 5)
                for i in range(NL):
                    eng.wait_ge(S["gBAS"], i + 1)
                    for k in range(NCH1):
                        idx = NCH1 * i + k
                        if DEBUG_TAPS and idx == 0:
                            eng.wait_ge(S["sY1"], 1)
                            eng.dma_start(out=dbg["y00"], in_=yb[:]).then_inc(S["qDBG"], 16)
                        eng.wait_ge(S["vPREP"], idx + 1)
                        if DEBUG_TAPS and idx == 0:
                            eng.dma_start(out=dbg["u0"], in_=ub[:]).then_inc(S["qDBG"], 16)
                            eng.dma_start(out=dbg["C0d"], in_=C0[:]).then_inc(S["qDBG"], 16)
                            eng.dma_start(out=dbg["S0d"], in_=S0[:]).then_inc(S["qDBG"], 16)
                            eng.dma_start(out=dbg["RC0d"], in_=RC0[:]).then_inc(S["qDBG"], 16)
                        eng.wait_ge(S["vSCU"], idx + 1)
                        eng.tensor_mul(q6[:], S0[:], Ub[:])
                        eng.wait_ge(S["vSCAN"], idx + 1)
                        if DEBUG_TAPS and idx == 0:
                            eng.dma_start(out=dbg["U0"], in_=Ub[:]).then_inc(S["qDBG"], 16)
                            eng.dma_start(out=dbg["V0"], in_=Vb[:]).then_inc(S["qDBG"], 16)
                        eng.tensor_mul(ub[:], C0[:], Vb[:])
                        eng.tensor_tensor(q6[:], ub[:], q6[:],
                                          ALU.subtract).then_inc(S["gCMB"], 1)
                    eng.wait_ge(S["vE"], i + 1)
                    eng.dma_start(out=cc_in, in_=ccs[:]).then_inc(S["qCCIN"], 16)
                    eng.wait_ge(S["qCCIN"], 16 * (i + 1))
                    eng.collective_compute(
                        "AllGather", ALU.bypass,
                        replica_groups=[[0, 1, 2, 3, 4, 5, 6, 7]],
                        ins=[cc_in], outs=[cc_out]).then_inc(S["cCC"], 1)
                    eng.wait_ge(S["cCC"], i + 1)
                    eng.dma_start(out=gth[:],
                                  in_=cc_out.rearrange("s p c -> p s c")
                                  ).then_inc(S["qGTH"], 16)
                    if DEBUG_TAPS and i == 0:
                        eng.wait_ge(S["vZ"], 1)
                        eng.dma_start(out=dbg["hc"], in_=h[:]).then_inc(S["qDBG"], 16)
                    if DEBUG_TAPS:
                        eng.wait_ge(S["vHF"], NCH1 * (i + 1))
                        eng.dma_start(out=dbg["hl"][i], in_=h[:]).then_inc(S["qOUT"], 16)
                eng.sem_inc(S["pB"], 1)

            blk1.sync(p1_sync)
            blk1.scalar(p1_scalar)
            blk1.vector(p1_vector)
            blk1.tensor(p1_tensor)
            blk1.gpsimd(p1_gpsimd)

        if STAGE < 3:
            return nc

        es2 = contextlib.ExitStack()
        with es2:
            E2 = es2.enter_context
            hL = E2(nc.sbuf_tensor("hL", [128, DB, TCORE], BF16))
            hb = E2(nc.sbuf_tensor("hb2", [128, DB, TC1], BF16))
            hqb = E2(nc.sbuf_tensor("hqb2", [128, DB, TC1], BF16))
            xh = E2(nc.sbuf_tensor("xh2", [128, DB, TC1], BF16))
            xnslab = E2(nc.sbuf_tensor("xnslab", [128, 4, DKB, TC0], BF16))
            p2stage = [E2(nc.sbuf_tensor(f"p2st{j}", [128, 2048], F32)) for j in range(2)]
            wbb = [E2(nc.sbuf_tensor(f"wbb{j}", [128, DKB * 128], BF16)) for j in range(2)]
            p2ost = [E2(nc.sbuf_tensor(f"p2os{j}", [128, DB * 128], F32)) for j in range(2)]
            wob = [E2(nc.sbuf_tensor(f"wob{j}", [128, DB * 128], BF16)) for j in range(2)]
            xres = [E2(nc.sbuf_tensor(f"xres{j}", [128, 2, TC0], F32)) for j in range(2)]
            outb = [E2(nc.sbuf_tensor(f"outb{j}", [128, 512], F32)) for j in range(2)]

            blk2 = E2(nc.Block("p2"))

            def p2_sync(eng):
                eng.wait_ge(S["pB"], 10)
                for sl in range(2):
                    for m in range(DKB):
                        gm = sl * DKB + m
                        if gm >= 1:
                            eng.wait_ge(S["vWC2"], 3 * (gm - 1) + 1)
                        eng.dma_start(out=p2stage[0][:],
                                      in_=wbase_t[m, :, 0:2048]
                                      ).then_inc(S["qP2A"], 16)
                        for ns in range(2):
                            g = gm * 2 + ns
                            if g >= 2:
                                eng.wait_ge(S["vEV2"], g - 1)
                            c2 = sl * 2 + ns
                            eng.dma_start(out=xres[g % 2][:],
                                          in_=xc[2 * c2:2 * c2 + 2, :, m, :]
                                          .transpose([1, 0, 2])
                                          ).then_inc(S[f"qPX{g % 2}"], 16)

            def p2_gpsimd(eng):
                eng.wait_ge(S["pB"], 10)
                for sl in range(2):
                    if sl > 0:
                        eng.wait_ge(S["tP2"], 64 * sl)
                    for q in range(4):
                        eng.dma_start(out=xnslab[:, q, :, :],
                                      in_=xn_scr[4 * sl + q]).then_inc(S["qXNL"], 16)
                    for m in range(DKB):
                        for ns in range(2):
                            g = (sl * DKB + m) * 2 + ns
                            eng.wait_ge(S["vEV2"], g + 1)
                            c2 = sl * 2 + ns
                            eng.dma_start(out=out_t[m, :, c2 * 512:(c2 + 1) * 512],
                                          in_=outb[g % 2][:]).then_inc(S["qOUT"], 16)
                eng.wait_ge(S["qOUT"], 16 * 128)

            def p2_compute(role, eng):
                eng.wait_ge(S["pB"], 10)
                def mk_ep3(tok):
                    def ep3(r_, e_):
                        if r_ == "vector":
                            for b in range(DB):
                                ins = e_.tensor_scalar(hL[:, b, tok], xh[:, b, :],
                                                       lnlws[:, b:b + 1],
                                                       lnlbs[:, b:b + 1],
                                                       ALU.mult, ALU.add)
                            ins.then_inc(S["vHLs"], 1)
                    return ep3

                if role == "vector":
                    seq = [(0, "copy"), (0, "rows"), (1, "copy"), (0, "norm"),
                           (1, "rows"), (2, "copy"), (1, "norm"), (2, "rows"),
                           (3, "copy"), (2, "norm"), (3, "rows"), (3, "norm")]
                    for kk, part in seq:
                        tok = slice(kk * TC1, (kk + 1) * TC1)
                        emit_ln(role, eng, 32 + kk, 0, h[:, :, tok],
                                mk_ep3(tok), part=part)
                else:
                    for kk in range(NCH1):
                        tok = slice(kk * TC1, (kk + 1) * TC1)
                        emit_ln(role, eng, 32 + kk, 0, h[:, :, tok], mk_ep3(tok))

                if role == "vector":
                    for gm in range(2 * DKB + 1):
                        if gm < 2 * DKB:
                            if gm >= 2:
                                eng.wait_ge(S["tP2"], 2 * (gm - 1))
                            for j_ in range(2):
                                eng.wait_ge(S["qP2A" if j_ == 0 else "qP2B"],
                                            16 * (gm + 1))
                                eng.tensor_copy(wbb[gm % 2][:, j_ * 2048:(j_ + 1) * 2048],
                                                p2stage[j_][:]).then_inc(S["vWC2"], 1)
                            eng.wait_ge(S["qP2O"], 16 * (gm + 1))
                            eng.tensor_copy(wob[gm % 2][:],
                                            p2ost[gm % 2][:]).then_inc(S["vWC2"], 1)
                        if gm >= 1:
                            gp = gm - 1
                            m_e = gp % DKB
                            for ns in range(2):
                                g = gp * 2 + ns
                                eng.wait_ge(S["tP2"], g + 1)
                                eng.wait_ge(S[f"qPX{g % 2}"], 16 * (g // 2 + 1))
                                if g >= 2:
                                    eng.wait_ge(S["qOUT"], 16 * g)
                                pbank = ps_mm[:, (g % 4) * 512:(g % 4 + 1) * 512]
                                ins = eng.scalar_tensor_tensor(
                                    outb[g % 2][:], pbank, bbs[:, m_e:m_e + 1],
                                    xres[g % 2][:].rearrange("p a b -> p (a b)"),
                                    ALU.add, ALU.add)
                                ins.then_inc(S["vEV2"], 1)
                if role == "tensor":
                    for sl in range(2):
                        for m in range(DKB):
                            gm = sl * DKB + m
                            eng.wait_ge(S["vWC2"], 3 * (gm + 1))
                            eng.wait_ge(S["qXNL"], 16 * 4 * (sl + 1))
                            for ns in range(2):
                                g = gm * 2 + ns
                                eng.wait_ge(S["vHLs"], sl * 2 + ns + 1)
                                if g >= 4:
                                    eng.wait_ge(S["vEV2"], g - 3)
                                pbank = ps_mm[:, (g % 4) * 512:(g % 4 + 1) * 512]
                                for kb in range(DB):
                                    eng.matmul(pbank,
                                               wob[gm % 2][:, kb * 128:(kb + 1) * 128],
                                               hL[:, kb, sl * 1024 + ns * 512:
                                                  sl * 1024 + (ns + 1) * 512],
                                               start=(kb == 0), stop=False)
                                for kb in range(DKB):
                                    eng.matmul(pbank,
                                               wbb[gm % 2][:, kb * 128:(kb + 1) * 128],
                                               xnslab[:, 2 * ns:2 * ns + 2, kb, :],
                                               start=False, stop=False)
                                c2 = sl * 2 + ns
                                mm = eng.matmul(pbank, bpb[:, m, :],
                                                hw[:, c2 * 512:(c2 + 1) * 512],
                                                start=False, stop=True)
                                mm.then_inc(S["tP2"], 1)

            def p2_scalar(eng):
                p2_compute("scalar", eng)
                for sl in range(2):
                    for m in range(DKB):
                        gm = sl * DKB + m
                        if gm >= 1:
                            eng.wait_ge(S["vWC2"], 3 * (gm - 1) + 2)
                        eng.dma_start(out=p2stage[1][:],
                                      in_=wbase_t[m, :, 2048:4096]
                                      ).then_inc(S["qP2B"], 16)
                        if gm >= 1:
                            eng.wait_ge(S["vWC2"], 3 * (gm - 1) + 3)
                        eng.dma_start(out=p2ost[gm % 2][:],
                                      in_=wout_t[:, m]).then_inc(S["qP2O"], 16)

            blk2.sync(p2_sync)
            blk2.scalar(p2_scalar)
            blk2.vector(lambda e: p2_compute("vector", e))
            blk2.tensor(lambda e: p2_compute("tensor", e))
            blk2.gpsimd(p2_gpsimd)

    return nc



def _tile_lhsT(W, K, M):
    Mb, Kb = M // 128, K // 128
    t = W.reshape(Mb, 128, Kb, 128).transpose(3, 2, 0, 1)
    return np.ascontiguousarray(t, dtype=np.float32)


def kernel(L0_attn_x, L0_attn_Win, L0_attn_bin, L0_attn_ln1_w, L0_attn_ln1_b,
           L0_attn_phz_re, L0_attn_phz_im, L0_attn_lc_re, L0_attn_lc_im,
           L0_attn_sc_lin, L0_attn_ln2_w, L0_attn_ln2_b, L0_attn_W1, L0_attn_b1,
           L0_attn_W2, L0_attn_b2, L0_attn_lnl_w, L0_attn_lnl_b, L0_attn_Wout,
           L0_attn_bout, L0_ffn_W_base, L0_ffn_b_base, L0_ffn_W_gate, L0_ffn_A,
           L0_ffn_B):
    x = np.asarray(L0_attn_x, dtype=np.float32).reshape(2 * 8192, D)

    def v128(v, n):
        return np.ascontiguousarray(np.asarray(v, np.float32).reshape(n, 128).T)

    win_tile = _tile_lhsT(np.asarray(L0_attn_Win, np.float32), D, DI).reshape(128, 16384)
    w1_tile = np.ascontiguousarray(np.stack(
        [_tile_lhsT(np.asarray(L0_attn_W1[i], np.float32), DI, FF).reshape(128, 8192)
         for i in range(NL)]))
    w2_tile = np.ascontiguousarray(np.stack(
        [_tile_lhsT(np.asarray(L0_attn_W2[i], np.float32), FF, DI).reshape(128, 8192)
         for i in range(NL)]))
    wout_km = _tile_lhsT(np.asarray(L0_attn_Wout, np.float32), DI, D)
    wout_tile = np.ascontiguousarray(wout_km.transpose(0, 2, 1, 3)).reshape(128, DKB, DB * 128)
    wb_km = _tile_lhsT(np.asarray(L0_ffn_W_base, np.float32), D, D)
    wbase_tile = np.ascontiguousarray(wb_km.transpose(2, 0, 1, 3)).reshape(DKB, 128, DKB * 128)

    Wg = np.asarray(L0_ffn_W_gate, np.float32)
    wg_tile = np.ascontiguousarray(Wg.T.reshape(DKB, 128, 8).transpose(1, 0, 2))
    Ap = np.asarray(L0_ffn_A, np.float32).transpose(1, 0, 2).reshape(128, D)
    a_tile = np.ascontiguousarray(Ap.T.reshape(DKB, 128, 128).transpose(1, 0, 2))
    Bp = np.asarray(L0_ffn_B, np.float32).transpose(2, 0, 1).reshape(128, D)
    bp_tile = np.ascontiguousarray(Bp.reshape(128, DKB, 128))

    pars = [L0_attn_ln1_w, L0_attn_ln1_b, L0_attn_phz_re, L0_attn_phz_im,
            L0_attn_lc_re, L0_attn_lc_im, L0_attn_sc_lin, L0_attn_ln2_w,
            L0_attn_ln2_b, L0_attn_b2]
    pch_arr = np.zeros((128, NL, NPAR, DB), np.float32)
    for j, par in enumerate(pars):
        a = np.asarray(par, np.float32)
        pch_arr[:, :, j, :] = a.reshape(NL, DB, 128).transpose(2, 0, 1)
    pch_arr = np.ascontiguousarray(pch_arr.reshape(128, NL * NPAR * DB))
    b1_arr = np.ascontiguousarray(
        np.asarray(L0_attn_b1, np.float32).reshape(NL, FB, 128)
        .transpose(2, 0, 1).reshape(128, NL * FB))

    common = dict(
        win_t=win_tile, w1_t=w1_tile, w2_t=w2_tile, wout_t=wout_tile,
        wbase_t=wbase_tile, wg_t=wg_tile, a_t=a_tile, bp_t=bp_tile,
        pch=pch_arr, b1p=b1_arr,
        binp=v128(L0_attn_bin, DB), lnlwi=v128(L0_attn_lnl_w, DB),
        lnlbi=v128(L0_attn_lnl_b, DB), bbp=v128(L0_ffn_b_base, DKB),
        bop=v128(L0_attn_bout, DKB), identi=np.eye(128, dtype=np.float32),
        sel8i=np.ascontiguousarray(
            (np.arange(128)[None, :] % 8 == np.arange(8)[:, None])
            .astype(np.float32)),
    )

    in_maps = []
    for c in range(NCORES):
        xcpart = x[c * TCORE:(c + 1) * TCORE]
        xc_t = xcpart.reshape(NCH0, TC0, DKB, 128).transpose(0, 3, 2, 1)
        g = c % 4
        mk = np.zeros((128, 29), np.float32)
        mk[:, 0] = 1.0 if g == 0 else 0.0
        mk[:, 1 + g] = 1.0
        base = 4 * (c // 4)
        for k in range(3):
            s = base + g - 1 - k
            if s >= base:
                mk[:, 5 + s * 3 + k] = 1.0
        m = dict(common)
        m["xc"] = np.ascontiguousarray(xc_t, dtype=np.float32)
        m["mki"] = mk
        in_maps.append(m)

    nc = build_nc()
    res = run_bass_kernel_spmd(nc, in_maps, core_ids=list(range(NCORES)))
    global _LAST_RES
    _LAST_RES = res
    outs = []
    for c in range(NCORES):
        arr = np.asarray(res.results[c]["out"])
        outs.append(arr.transpose(2, 0, 1).reshape(TCORE, D))
    return np.concatenate(outs, axis=0).reshape(2, 8192, D).astype(np.float32)
